# version 10
# baseline (speedup 1.0000x reference)
import numpy as np
import concourse.bass as bass
import concourse.mybir as mybir
from concourse.bass_utils import run_bass_kernel_spmd
from concourse.bass_core_rust import IndirectOffsetOnAxis
from contextlib import ExitStack

F32 = mybir.dt.float32
BF16 = mybir.dt.bfloat16
I32 = mybir.dt.int32
AF = mybir.ActivationFunctionType
ALU = mybir.AluOpType
AX = mybir.AxisListType

B, SQ, H = 4, 4096, 2048
NH, HD = 16, 128
NORM = 1.0 / float(np.sqrt(HD))
IFF = 8192
EPS = 1e-5
TPC = 2048
NCORE = 8

ENGS = ("sync", "gpsimd", "vector", "scalar", "tensor")
ARENA_BASE = 17408
ARENA_TOP = 229376


class Sched:

    def __init__(self):
        self.ops = []
        self.last_w = {}
        self.readers = {}
        self.fence_pend = None
        self.fence_seen = set()

    def op(self, eng, fn, reads=(), writes=(), dma=False, cc=False, tag=""):
        idx = len(self.ops)
        deps = set()
        for r in reads:
            if r in self.last_w:
                deps.add((self.last_w[r], "raw"))
        for w in writes:
            if w in self.last_w:
                deps.add((self.last_w[w], "waw"))
            for rd in self.readers.get(w, ()):
                deps.add((rd, "war"))
        if self.fence_pend is not None and eng not in self.fence_seen:
            for j in self.fence_pend:
                deps.add((j, "fence"))
            self.fence_seen.add(eng)
        o = dict(idx=idx, eng=eng, fn=fn, dma=dma, cc=cc, deps=deps,
                 needed=False, tag=tag)
        self.ops.append(o)
        for r in reads:
            self.readers.setdefault(r, []).append(idx)
        for w in writes:
            self.last_w[w] = idx
            self.readers[w] = []
        return idx

    def fence(self):
        last_per_eng = {}
        for o in self.ops:
            last_per_eng[o["eng"]] = o["idx"]
        self.fence_pend = set(last_per_eng.values())
        self.fence_seen = set()

    def _edge_needed(self, p, c, typ):
        if p["cc"] or c["cc"]:
            return True
        if p["eng"] == c["eng"]:
            if typ == "fence":
                return False
            if p["dma"] and c["dma"]:
                return False
            if p["eng"] == "tensor" and not p["dma"] and not c["dma"]:
                return False
            if typ == "war":
                return p["dma"]
            return True
        return True

    def emit(self, block, sems_c, sems_d, ccsem):
        ops = self.ops
        for o in ops:
            fdeps = []
            for (j, typ) in o["deps"]:
                p = ops[j]
                if self._edge_needed(p, o, typ):
                    fdeps.append(j)
                    p["needed"] = True
            o["fdeps"] = fdeps
        cum = {}
        dma_count = {}
        for o in ops:
            if o["dma"]:
                o["needed"] = True
            if o["cc"]:
                nc_cc = cum.get("cc", 0)
                cum["cc"] = nc_cc + 1
                o["semkey"] = ("cc", nc_cc)
                o["semval"] = 1
                o["prewait"] = None
            elif o["dma"]:
                eng = o["eng"]
                n = dma_count.get(eng, 0)
                dma_count[eng] = n + 1
                pool = len(sems_d[eng])
                slot = n % pool
                key = (eng, "dma", slot)
                prev = cum.get(key, 0)
                o["prewait"] = (key, prev) if prev > 0 else None
                cum[key] = prev + 16
                o["semkey"] = key
                o["semval"] = cum[key]
            elif o["needed"]:
                key = (eng_k := o["eng"], "c")
                cum[key] = cum.get(key, 0) + 1
                o["semkey"] = key
                o["semval"] = cum[key]
                o["prewait"] = None
            else:
                o["semkey"] = None
                o["prewait"] = None

        def sem_of(key):
            if key[0] == "cc":
                return ccsem[key[1]]
            if key[1] == "dma":
                return sems_d[key[0]][key[2]]
            return sems_c[key[0]]

        for o in ops:
            waits = {}
            for j in o["fdeps"]:
                p = ops[j]
                k, v = p["semkey"], p["semval"]
                assert k is not None, (o["tag"], p["tag"])
                if k not in waits or waits[k] < v:
                    waits[k] = v
            if o.get("prewait"):
                k, v = o["prewait"]
                if k not in waits or waits[k] < v:
                    waits[k] = v
            o["waits"] = [(sem_of(k), v) for k, v in waits.items()]

        def body(eng_name):
            def run(eng):
                watermark = {}
                for o in ops:
                    if o["eng"] != eng_name:
                        continue
                    for (s, v) in o["waits"]:
                        if watermark.get(id(s), 0) < v:
                            eng.wait_ge(s, v)
                            watermark[id(s)] = v
                    inst = o["fn"](eng)
                    if o["cc"]:
                        inst.then_inc(ccsem[o["semkey"][1]])
                    elif o["needed"]:
                        inst.then_inc(sem_of(o["semkey"]), 16 if o["dma"] else 1)
            return run

        block.sync(body("sync"))
        block.gpsimd(body("gpsimd"))
        block.vector(body("vector"))
        block.scalar(body("scalar"))
        block.tensor(body("tensor"))


def build_program(dbg=False):
    nc = bass.Bass("TRN2", target_bir_lowering=False, debug=False)

    xs = nc.dram_tensor("xs", [SQ, H], F32, kind="ExternalInput").ap()
    xr = nc.dram_tensor("xr", [TPC, H], F32, kind="ExternalInput").ap()
    wq_st = nc.dram_tensor("wq_st", [8, 3, 128, 16, 128], F32, kind="ExternalInput").ap()
    wp_st = nc.dram_tensor("wp_st", [16, 128, 16, 128], F32, kind="ExternalInput").ap()
    gw_st = nc.dram_tensor("gw_st", [64, 128, 16, 128], I32, kind="ExternalInput").ap()
    uw_st = nc.dram_tensor("uw_st", [64, 128, 16, 128], I32, kind="ExternalInput").ap()
    dw_st = nc.dram_tensor("dw_st", [64, 128, 2048], I32, kind="ExternalInput").ap()
    gs_in = nc.dram_tensor("gs_in", [128, 64], F32, kind="ExternalInput").ap()
    us_in = nc.dram_tensor("us_in", [128, 64], F32, kind="ExternalInput").ap()
    ds_in = nc.dram_tensor("ds_in", [1, 2048], F32, kind="ExternalInput").ap()
    bp_in = nc.dram_tensor("bp_in", [128, 16], F32, kind="ExternalInput").ap()
    cg_in = nc.dram_tensor("cg_in", [128, 16], I32, kind="ExternalInput").ap()
    y_out = nc.dram_tensor("y_out", [TPC, H], F32, kind="ExternalOutput").ap()

    KD = "ExternalOutput" if dbg else "Internal"
    q_scr = nc.dram_tensor("q_scr", [8, 128, SQ], BF16, kind=KD).ap()
    k_scr = nc.dram_tensor("k_scr", [8, 128, SQ], BF16, kind=KD).ap()
    v_scr = nc.dram_tensor("v_scr", [8, 128, 32, 128], BF16, kind=KD).ap()
    gu_scr = nc.dram_tensor("gu_scr", [64, 2, 128, 2048], BF16, kind="Internal").ap()
    dbf_scr = nc.dram_tensor("dbf_scr", [64, 128, 2048], BF16, kind="Internal").ap()
    y_scr = nc.dram_tensor("y_scr", [TPC, H], F32, kind=KD).ap()
    ctx_loc = nc.dram_tensor("ctx_loc", [8 * 128, SQ], BF16, kind=KD).ap()
    ctx_gath = [nc.dram_tensor(f"ctx_gath{h}", [NCORE * 4 * 128, SQ], BF16,
                               kind="Internal", addr_space="Shared").ap()
                for h in range(2)]
    ctx_gview = [g.rearrange("r (h c) -> (r h) c", h=2) for g in ctx_gath]

    cur = [ARENA_BASE]

    def alloc(name, shape, dt, at=None):
        if at is None:
            at = cur[0]
        szmap = {F32: 4, BF16: 2, I32: 4}
        free_elems = int(np.prod(shape[1:])) if len(shape) > 1 else shape[0]
        nbytes = free_elems * szmap[dt]
        t = nc.alloc_sbuf_tensor_at(name, list(shape), dt, offset=at).ap()
        end = at + ((nbytes + 31) // 32) * 32
        cur[0] = max(cur[0], end)
        assert end <= ARENA_TOP, (name, at, end)
        return t

    ident = alloc("ident", [128, 128], BF16)
    ident_f = alloc("ident_f", [128, 128], F32)
    ones_col = alloc("ones_col", [128, 1], BF16)
    ones_row = alloc("ones_row", [1, 128], BF16)
    ones_row_f = alloc("ones_row_f", [1, 128], F32)
    m128row = alloc("m128row", [1, 128], BF16)
    gs_sb = alloc("gs_sb", [128, 64], F32)
    us_sb = alloc("us_sb", [128, 64], F32)
    bp_sb = alloc("bp_sb", [128, 16], F32)
    cgi_sb = alloc("cgi_sb", [128, 16], I32)
    ds_row = alloc("ds_row", [1, 2048], F32)
    dsb = alloc("dsb", [128, 2048], F32)
    stats = [alloc(f"stat{i}", [128, 16], F32) for i in range(2)]
    srow_f = alloc("srow_f", [1, 512], F32)
    srow_b = alloc("srow_b", [1, 512], BF16)
    lrow_f = alloc("lrow_f", [1, 512], F32)
    linv_f = alloc("linv_f", [1, 512], F32)
    linv_b = alloc("linv_b", [1, 512], BF16)
    gstat = alloc("gstat", [1, 16], F32)
    gstat_b = alloc("gstat_b", [1, 16], BF16)
    negc = alloc("negc", [128, 2], F32)
    eps_sb = alloc("eps_sb", [128, 1], F32)
    e20_sb = alloc("e20_sb", [128, 1], F32)
    PH = ((cur[0] + 127) // 128) * 128

    ln1fm = alloc("ln1fm", [128, 16, SQ], BF16, at=PH)
    P1A = PH + 131072
    xbuf = [alloc(f"xbuf{i}", [128, 2048], F32, at=P1A + i * 8192) for i in range(2)]
    ln1tm = [alloc(f"ln1tm{i}", [128, 2048], F32, at=P1A + 16384 + i * 8192) for i in range(2)]
    wqf = [alloc(f"wqf{i}", [128, 2048], F32, at=P1A + i * 8192) for i in range(2)]
    wqb = [alloc(f"wqb{i}", [128, 2048], BF16, at=P1A + 16384 + i * 4096) for i in range(2)]
    qkvb = [alloc(f"qkvb{i}", [128, SQ], BF16, at=P1A + 24576 + i * 8192) for i in range(2)]
    vtmb = alloc("vtmb", [128, 32, 128], BF16, at=P1A + 40960)
    vstg = [alloc(f"vstg{i}", [128, 512], F32, at=P1A + 49152 + i * 2048) for i in range(2)]

    ptile = [alloc(f"ptile{i}", [128, 32, 512], BF16, at=PH + i * 32768) for i in range(2)]
    qsb = [alloc(f"qsb{i}", [128, SQ], BF16, at=PH + 65536 + i * 8192) for i in range(2)]
    ksb = [alloc(f"ksb{i}", [128, SQ], BF16, at=PH + 81920 + i * 8192) for i in range(2)]
    vsb = [alloc(f"vsb{i}", [128, 32, 128], BF16, at=PH + 98304 + i * 8192) for i in range(2)]
    q2b = alloc("q2b", [128, SQ], BF16, at=PH + 114688)
    ctxo = [alloc(f"ctxo{i}", [128, SQ], BF16, at=PH + 122880 + i * 8192) for i in range(2)]
    lbc = [alloc(f"lbc{i}", [128, 512], BF16, at=PH + 139264 + i * 1024) for i in range(2)]
    cvin = [alloc(f"cvin{i}", [128, 2048], I32, at=PH + 141312 + i * 8192) for i in range(2)]
    cvbf = [alloc(f"cvbf{i}", [128, 2048], BF16, at=PH + 157696 + i * 4096) for i in range(2)]
    cvtmp = alloc("cvtmp", [128, 2048], F32, at=PH + 165888)

    ctxin = alloc("ctxin", [128, 16, 2048], BF16, at=PH)
    zfm = alloc("zfm", [128, 16, 2048], BF16, at=PH + 65536)
    wpf = [alloc(f"wpf{i}", [128, 2048], F32, at=PH + 131072 + i * 8192) for i in range(2)]
    wpb = [alloc(f"wpb{i}", [128, 2048], BF16, at=PH + 147456 + i * 4096) for i in range(2)]
    ln2fm = alloc("ln2fm", [128, 16, 2048], BF16, at=PH)
    t1b = alloc("t1b", [128, 2048], F32, at=PH + 131072)
    ln2tm = alloc("ln2tm", [128, 2048], F32, at=PH + 139264)
    atm = alloc("atm", [128, 2048], BF16, at=PH + 147456)
    zstg = [alloc(f"zstg{i}", [128, 512], F32, at=PH + 151552 + i * 2048) for i in range(2)]
    xres = [alloc(f"xres{i}", [128, 2048], F32, at=PH + 155648 + i * 8192) for i in range(2)]
    ybuf = alloc("ybuf", [128, 2048], F32, at=PH + 172032)

    h2 = alloc("h2", [128, 64, 512], BF16, at=PH + 65536)
    gubuf = [alloc(f"gubuf{i}", [128, 2, 2048], BF16, at=PH + 131072 + i * 8192) for i in range(2)]
    dbuf = [alloc(f"dbuf{i}", [128, 2048], BF16, at=PH + 147456 + i * 4096) for i in range(2)]
    gstg = alloc("gstg", [128, 512], F32, at=PH + 155648)
    ustg = alloc("ustg", [128, 512], F32, at=PH + 157696)
    ftm = [alloc(f"ftm{i}", [128, 2048], BF16, at=PH + 159744 + i * 4096) for i in range(2)]
    yresb = alloc("yresb", [128, 2048], F32, at=PH + 167936)
    outb = alloc("outb", [128, 2048], F32, at=PH + 176128)

    SPD = [nc.alloc_psum_tensor(f"spd{i}", [128, 1024], F32).ap() for i in range(2)]
    CT = [nc.alloc_psum_tensor(f"ct{i}", [128, 512], F32).ap() for i in range(2)]
    LP = nc.alloc_psum_tensor("lp", [128, 512], F32).ap()
    BCp = nc.alloc_psum_tensor("bc", [128, 512], F32).ap()
    SP = [SPD[0][:, 0:512], SPD[0][:, 512:1024],
          SPD[1][:, 0:512], SPD[1][:, 512:1024]]
    ACC8 = SP + CT + [LP, BCp]

    S = Sched()
    PS = lambda name: ("ps", name)

    def ln_stats(x_ap, st, pfx, sqdump, inkeys, stoff=0, n=2048, stbuf=None,
                 sqkey=("sqd",)):
        key = ("st", stbuf, stoff)
        c = lambda i: slice(stoff + i, stoff + i + 1)
        S.op("vector", lambda v, a=x_ap, o=st: v.reduce_sum(o[:, c(0)], a, axis=AX.X),
             reads=inkeys, writes=[key + (0,)], tag=pfx + ".sum")
        S.op("scalar", lambda s, a=x_ap, o=st, d=sqdump: s.activation(
            out=d[:, :], in_=a, func=AF.Square, accum_out=o[:, c(1)]),
             reads=inkeys, writes=[key + (1,), sqkey], tag=pfx + ".sq")
        S.op("vector", lambda v, o=st: v.tensor_scalar(
            out=o[:, c(0)], in0=o[:, c(0)], scalar1=1.0 / n, scalar2=None, op0=ALU.mult),
             reads=[key + (0,)], writes=[key + (0,)], tag=pfx + ".mean")
        S.op("vector", lambda v, o=st: v.tensor_scalar(
            out=o[:, c(1)], in0=o[:, c(1)], scalar1=1.0 / n, scalar2=None, op0=ALU.mult),
             reads=[key + (1,)], writes=[key + (1,)], tag=pfx + ".msq")
        S.op("vector", lambda v, o=st: v.tensor_tensor(
            out=o[:, c(3)], in0=o[:, c(0)], in1=o[:, c(0)], op=ALU.mult),
             reads=[key + (0,)], writes=[key + (3,)], tag=pfx + ".m2")
        S.op("vector", lambda v, o=st: v.tensor_tensor(
            out=o[:, c(1)], in0=o[:, c(1)], in1=o[:, c(3)], op=ALU.subtract),
             reads=[key + (1,), key + (3,)], writes=[key + (1,)], tag=pfx + ".var")
        S.op("scalar", lambda s, o=st: s.activation(
            out=o[:, c(2)], in_=o[:, c(1)], func=AF.Ln, bias=eps_sb[:, 0:1], scale=1.0),
             reads=[key + (1,), ("epsb",)], writes=[key + (2,)], tag=pfx + ".ln")
        S.op("scalar", lambda s, o=st: s.activation(
            out=o[:, c(2)], in_=o[:, c(2)], func=AF.Exp, bias=0.0, scale=-0.5),
             reads=[key + (2,)], writes=[key + (2,)], tag=pfx + ".exp")
        return key

    S.op("gpsimd", lambda g: g.memset(ident[:, :], 0.0), writes=[("id0",)], tag="id0")
    S.op("gpsimd", lambda g: g.affine_select(
        out=ident[:, :], in_=ident[:, :], compare_op=ALU.not_equal, fill=1.0,
        base=0, pattern=[[-1, 128]], channel_multiplier=1),
         reads=[("id0",)], writes=[("ident",)], tag="id1")
    S.op("gpsimd", lambda g: g.memset(ident_f[:, :], 0.0), writes=[("idf0",)], tag="idf0")
    S.op("gpsimd", lambda g: g.affine_select(
        out=ident_f[:, :], in_=ident_f[:, :], compare_op=ALU.not_equal, fill=1.0,
        base=0, pattern=[[-1, 128]], channel_multiplier=1),
         reads=[("idf0",)], writes=[("identf",)], tag="idf1")
    S.op("gpsimd", lambda g: g.memset(ones_col[:, :], 1.0), writes=[("onec",)], tag="onec")
    S.op("gpsimd", lambda g: g.memset(ones_row[:, :], 1.0), writes=[("oner",)], tag="oner")
    S.op("gpsimd", lambda g: g.memset(ones_row_f[:, :], 1.0), writes=[("onerf",)], tag="onerf")
    S.op("gpsimd", lambda g: g.memset(m128row[:, :], -128.0), writes=[("m128",)], tag="m128")
    S.op("gpsimd", lambda g: g.memset(eps_sb[:, :], EPS), writes=[("epsb",)], tag="epsb")
    S.op("gpsimd", lambda g: g.memset(e20_sb[:, :], 1e-20), writes=[("e20b",)], tag="e20b")
    S.op("sync", lambda e: e.dma_start(out=gs_sb[:, :], in_=gs_in), writes=[("gs",)], dma=True, tag="ld.gs")
    S.op("sync", lambda e: e.dma_start(out=us_sb[:, :], in_=us_in), writes=[("us",)], dma=True, tag="ld.us")
    S.op("sync", lambda e: e.dma_start(out=bp_sb[:, :], in_=bp_in), writes=[("bp",)], dma=True, tag="ld.bp")
    S.op("sync", lambda e: e.dma_start(out=cgi_sb[:, :], in_=cg_in), writes=[("cgi",)], dma=True, tag="ld.cg")
    S.op("sync", lambda e: e.dma_start(out=ds_row[:, :], in_=ds_in), writes=[("dsr",)], dma=True, tag="ld.ds")
    for j in range(4):
        S.op("tensor", lambda t, j=j: t.matmul(
            BCp[:, :], ones_row_f[:, :], ds_row[0:1, j * 512:(j + 1) * 512],
            start=True, stop=True),
             reads=[("onerf",), ("dsr",)], writes=[PS("bc")], tag=f"dsb.mm{j}")
        S.op("vector", lambda v, j=j: v.tensor_copy(
            out=dsb[:, j * 512:(j + 1) * 512], in_=BCp[:, :]),
             reads=[PS("bc")], writes=[("dsb",)], tag=f"dsb.cp{j}")

    for tt in range(32):
        p = tt % 2
        S.op("sync", lambda e, tt=tt, p=p: e.dma_start(
            out=xbuf[p][:, :], in_=xs[tt * 128:(tt + 1) * 128, :]),
             writes=[("xb", p)], dma=True, tag=f"p1.x{tt}")
        pfx = "ln1_" + str(tt)
        key = ln_stats(xbuf[p][:, :], stats[p], pfx, ln1tm[p], [("xb", p)], stbuf=p,
                       sqkey=("ln1tm", p))
        S.op("vector", lambda v, p=p: v.tensor_scalar(
            out=ln1tm[p][:, :], in0=xbuf[p][:, :],
            scalar1=stats[p][:, 0:1], scalar2=stats[p][:, 2:3],
            op0=ALU.subtract, op1=ALU.mult),
             reads=[("xb", p), key + (0,), key + (2,)],
             writes=[("ln1tm", p)], tag=f"p1.nrm{tt}")
        for q in range(4):
            tp = q % 2
            for j in range(4):
                hc = q * 4 + j
                S.op("tensor", lambda t, p=p, hc=hc, j=j, tp=tp: t.transpose(
                    SP[tp][:, j * 128:(j + 1) * 128],
                    ln1tm[p][:, hc * 128:(hc + 1) * 128], ident_f[:, :]),
                     reads=[("ln1tm", p), ("identf",)], writes=[PS("sp%d" % tp)],
                     tag=f"p1.tp{tt}.{hc}")
            S.op("vector", lambda v, p=p, q=q, tt=tt, tp=tp: v.tensor_copy(
                out=ln1fm[:, q * 4:(q + 1) * 4, tt * 128:(tt + 1) * 128],
                in_=SP[tp].rearrange("p (j f) -> p j f", j=4)),
                 reads=[PS("sp%d" % tp)], writes=[("ln1fm", tt)],
                 tag=f"p1.ev{tt}.{q}")

    S.fence()

    LN1ALL = [("ln1fm", tt) for tt in range(32)]
    for hl in range(8):
        for kind in range(3):
            bi = (hl * 3 + kind) % 2
            S.op("sync", lambda e, hl=hl, kind=kind, bi=bi: e.dma_start(
                out=wqf[bi][:, :], in_=wq_st[hl, kind]),
                 writes=[("wqf", bi)], dma=True, tag=f"p1.wld{hl}.{kind}")
            S.op("vector", lambda v, bi=bi: v.tensor_copy(
                out=wqb[bi][:, :], in_=wqf[bi][:, :]),
                 reads=[("wqf", bi)], writes=[("wqb", bi)], tag=f"p1.wcv{hl}.{kind}")
            ob = bi
            for tc in range(8):
                sp = tc % 2
                for hc in range(16):
                    S.op("tensor", lambda t, bi=bi, hc=hc, tc=tc, sp=sp: t.matmul(
                        SP[sp][:, :], wqb[bi][:, hc * 128:(hc + 1) * 128],
                        ln1fm[:, hc, tc * 512:(tc + 1) * 512],
                        start=(hc == 0), stop=(hc == 15)),
                         reads=[("wqb", bi)] + LN1ALL, writes=[PS("sp%d" % sp)],
                         tag=f"p1.mm{hl}.{kind}.{tc}.{hc}")
                S.op("vector", lambda v, ob=ob, tc=tc, sp=sp: v.tensor_copy(
                    out=qkvb[ob][:, tc * 512:(tc + 1) * 512], in_=SP[sp][:, :]),
                     reads=[PS("sp%d" % sp)], writes=[("qkvb", ob, tc)],
                     tag=f"p1.qev{hl}.{kind}.{tc}")
            QK_ALL = [("qkvb", ob, tc) for tc in range(8)]
            if kind == 0:
                S.op("scalar", lambda e, hl=hl, ob=ob: e.dma_start(
                    out=q_scr[hl], in_=qkvb[ob][:, :]),
                     reads=QK_ALL, writes=[("qscr", hl)], dma=True, tag=f"p1.qst{hl}")
            elif kind == 1:
                S.op("scalar", lambda e, hl=hl, ob=ob: e.dma_start(
                    out=k_scr[hl], in_=qkvb[ob][:, :]),
                     reads=QK_ALL, writes=[("kscr", hl)], dma=True, tag=f"p1.kst{hl}")
            else:
                for q in range(8):
                    tp = q % 2
                    S.op("vector", lambda v, q=q, tp=tp, ob=ob: v.tensor_copy(
                        out=vstg[tp][:, :], in_=qkvb[ob][:, q * 512:(q + 1) * 512]),
                         reads=[("qkvb", ob, q)], writes=[("vstg", tp)],
                         tag=f"p1.vsg{hl}.{q}")
                    for j in range(4):
                        S.op("tensor", lambda t, j=j, tp=tp: t.transpose(
                            SP[2 + tp][:, j * 128:(j + 1) * 128],
                            vstg[tp][:, j * 128:(j + 1) * 128], ident_f[:, :]),
                             reads=[("vstg", tp), ("identf",)],
                             writes=[PS("sp%d" % (2 + tp))], tag=f"p1.vtp{hl}.{q}.{j}")
                    S.op("vector", lambda v, q=q, tp=tp: v.tensor_copy(
                        out=vtmb[:, q * 4:(q + 1) * 4, :],
                        in_=SP[2 + tp].rearrange("p (j f) -> p j f", j=4)),
                         reads=[PS("sp%d" % (2 + tp))], writes=[("vtmb", q)],
                         tag=f"p1.vev{hl}.{q}")
                S.op("scalar", lambda e, hl=hl: e.dma_start(
                    out=v_scr[hl], in_=vtmb[:, :, :]),
                     reads=[("vtmb", q) for q in range(8)],
                     writes=[("vscr", hl)], dma=True, tag=f"p1.vst{hl}")

    S.fence()

    def emit_group_load(g):
        p = g % 2
        S.op("sync", lambda e, g=g, p=p: e.dma_start(out=qsb[p][:, :], in_=q_scr[g]),
             reads=[("qscr", g)], writes=[("qsb", p)], dma=True, tag=f"p2.qld{g}")
        S.op("sync", lambda e, g=g, p=p: e.dma_start(out=ksb[p][:, :], in_=k_scr[g]),
             reads=[("kscr", g)], writes=[("ksb", p)], dma=True, tag=f"p2.kld{g}")
        S.op("sync", lambda e, g=g, p=p: e.dma_start(out=vsb[p][:, :, :], in_=v_scr[g]),
             reads=[("vscr", g)], writes=[("vsb", p)], dma=True, tag=f"p2.vld{g}")

    def emit_norm_bound(g, p, src_key, sq_tag, out_col, mkey):
        buf = qsb[p] if src_key == "q" else ksb[p]
        S.op("scalar", lambda s, buf=buf: s.activation(
            out=q2b[:, :], in_=buf[:, :], func=AF.Square),
             reads=[(src_key + "sb", p)], writes=[("q2b",)], tag=f"p2.sq_{sq_tag}{g}")
        for j in range(8):
            S.op("tensor", lambda t, j=j: t.matmul(
                LP[0:1, :], ones_col[:, :], q2b[:, j * 512:(j + 1) * 512],
                start=True, stop=True),
                 reads=[("q2b",), ("onec",)], writes=[PS("lp")], tag=f"p2.nn_{sq_tag}{g}.{j}")
            S.op("vector", lambda v: v.reduce_max(
                gstat[0:1, 8:9], LP[0:1, :], axis=AX.X),
                 reads=[PS("lp")], writes=[("gred",)], tag=f"p2.nm_{sq_tag}{g}.{j}")
            if j == 0:
                S.op("vector", lambda v, oc=out_col: v.tensor_copy(
                    out=gstat[0:1, oc:oc + 1], in_=gstat[0:1, 8:9]),
                     reads=[("gred",)], writes=[mkey], tag=f"p2.n0_{sq_tag}{g}")
            else:
                S.op("vector", lambda v, oc=out_col: v.tensor_tensor(
                    out=gstat[0:1, oc:oc + 1], in0=gstat[0:1, oc:oc + 1],
                    in1=gstat[0:1, 8:9], op=ALU.max),
                     reads=[("gred",), mkey], writes=[mkey], tag=f"p2.nx_{sq_tag}{g}.{j}")

    def emit_conv_ic(ic):
        for kindc, src_t, slot in (("g", gw_st, 0), ("u", uw_st, 1)):
            p = (2 * ic + slot) % 2
            S.op("gpsimd", lambda e, src_t=src_t, ic=ic, p=p: e.dma_start(
                out=cvin[p][:, :], in_=src_t[ic]),
                 writes=[("cvin", p)], dma=True, tag=f"cv.{kindc}l{ic}")
            S.op("vector", lambda v, p=p: v.tensor_copy(
                out=cvbf[p][:, :], in_=cvin[p][:, :]),
                 reads=[("cvin", p)], writes=[("cvbf", p)], tag=f"cv.{kindc}c{ic}")
            S.op("gpsimd", lambda e, ic=ic, p=p, slot=slot: e.dma_start(
                out=gu_scr[ic, slot], in_=cvbf[p][:, :]),
                 reads=[("cvbf", p)], writes=[("guscr", ic, slot)], dma=True,
                 tag=f"cv.{kindc}s{ic}")
        p = ic % 2
        S.op("gpsimd", lambda e, ic=ic, p=p: e.dma_start(
            out=cvin[p][:, :], in_=dw_st[ic]),
             writes=[("cvin", p)], dma=True, tag=f"cv.dl{ic}")
        S.op("vector", lambda v, p=p: v.tensor_scalar(
            out=cvtmp[:, :], in0=cvin[p][:, :], scalar1=128.0, scalar2=None,
            op0=ALU.subtract),
             reads=[("cvin", p)], writes=[("cvtmp",)], tag=f"cv.dsub{ic}")
        S.op("vector", lambda v, p=p: v.tensor_tensor(
            out=cvbf[p][:, :], in0=cvtmp[:, :], in1=dsb[:, :], op=ALU.mult),
             reads=[("cvtmp",), ("dsb",)], writes=[("cvbf", p)], tag=f"cv.dd{ic}")
        S.op("gpsimd", lambda e, ic=ic, p=p: e.dma_start(
            out=dbf_scr[ic], in_=cvbf[p][:, :]),
             reads=[("cvbf", p)], writes=[("dscr", ic)], dma=True,
             tag=f"cv.ds{ic}")

    emit_group_load(0)
    for g in range(8):
        p = g % 2
        if g + 1 < 8:
            emit_group_load(g + 1)
        emit_norm_bound(g, p, "q", "q", 0, ("m2q",))
        emit_norm_bound(g, p, "k", "k", 1, ("m2k",))
        S.op("vector", lambda v: v.tensor_tensor(
            out=gstat[0:1, 4:5], in0=gstat[0:1, 0:1], in1=gstat[0:1, 1:2], op=ALU.mult),
             reads=[("m2q",), ("m2k",)], writes=[("mprod",)], tag=f"p2.cp{g}")
        S.op("scalar", lambda s: s.activation(
            out=gstat[0:1, 4:5], in_=gstat[0:1, 4:5], func=AF.Ln, bias=e20_sb[0:1, 0:1], scale=1.0),
             reads=[("mprod",), ("e20b",)], writes=[("mprod",)], tag=f"p2.cln{g}")
        S.op("scalar", lambda s: s.activation(
            out=gstat[0:1, 4:5], in_=gstat[0:1, 4:5], func=AF.Exp, bias=0.0, scale=0.5),
             reads=[("mprod",)], writes=[("mprod",)], tag=f"p2.cex{g}")
        S.op("vector", lambda v: v.tensor_scalar(
            out=gstat_b[0:1, 0:1], in0=gstat[0:1, 4:5], scalar1=-NORM, scalar2=None,
            op0=ALU.mult),
             reads=[("mprod",)], writes=[("negc1",)], tag=f"p2.cng{g}")
        S.op("tensor", lambda t: t.matmul(
            BCp[:, 0:1], ones_row[:, :], gstat_b[0:1, 0:1], start=True, stop=True),
             reads=[("negc1",), ("oner",)], writes=[PS("bc")], tag=f"p2.cbc{g}")
        S.op("vector", lambda v, p=p: v.tensor_copy(out=negc[:, p:p + 1], in_=BCp[:, 0:1]),
             reads=[PS("bc")], writes=[("negc", p)], tag=f"p2.ccp{g}")

        def emit_smm(qc):
            par = qc % 2
            for kt in range(32):
                sp = kt % 4
                S.op("tensor", lambda t, p=p, kt=kt, qc=qc, sp=sp: t.matmul(
                    SP[sp][:, :], ksb[p][:, kt * 128:(kt + 1) * 128],
                    qsb[p][:, qc * 512:(qc + 1) * 512], start=True, stop=True),
                     reads=[("ksb", p), ("qsb", p)], writes=[PS("sp%d" % sp)],
                     tag=f"p2.smm{g}.{qc}.{kt}")
                if kt % 2 == 1:
                    di = (kt // 2) % 2
                    S.op("scalar", lambda s, par=par, kt=kt, di=di, p=p: s.activation(
                        out=ptile[par].rearrange("p a b -> p (a b)")[
                            :, (kt - 1) * 512:(kt + 1) * 512],
                        in_=SPD[di][:, :], func=AF.Exp,
                        bias=negc[:, p:p + 1], scale=NORM),
                         reads=[PS("sp%d" % (2 * di)), PS("sp%d" % (2 * di + 1)),
                                ("negc", p)],
                         writes=[("pt", par, kt - 1), ("pt", par, kt)],
                         tag=f"p2.exp{g}.{qc}.{kt}")

        def emit_tail(qc):
            par = qc % 2
            for kt in range(32):
                S.op("tensor", lambda t, par=par, kt=kt: t.matmul(
                    LP[0:1, :], ones_col[:, :], ptile[par][:, kt, :],
                    start=(kt == 0), stop=(kt == 31)),
                     reads=[("pt", par, kt), ("onec",)], writes=[PS("lp")],
                     tag=f"p2.lmm{g}.{qc}.{kt}")
            S.op("vector", lambda v: v.reciprocal(out=linv_f[0:1, :], in_=LP[0:1, :]),
                 reads=[PS("lp")], writes=[("linvf",)], tag=f"p2.lrc{g}.{qc}")
            S.op("vector", lambda v: v.tensor_copy(out=linv_b[0:1, :], in_=linv_f[0:1, :]),
                 reads=[("linvf",)], writes=[("linv",)], tag=f"p2.lrb{g}.{qc}")
            S.op("tensor", lambda t: t.matmul(
                BCp[:, :], ones_row[:, :], linv_b[0:1, :], start=True, stop=True),
                 reads=[("linv",), ("oner",)], writes=[PS("bc")], tag=f"p2.lbm{g}.{qc}")
            S.op("vector", lambda v, par=par: v.tensor_copy(out=lbc[par][:, :], in_=BCp[:, :]),
                 reads=[PS("bc")], writes=[("lbcs", par)], tag=f"p2.lbe{g}.{qc}")
            for kt in range(32):
                S.op("tensor", lambda t, p=p, par=par, kt=kt: t.matmul(
                    CT[par][:, :], vsb[p][:, kt, :], ptile[par][:, kt, :],
                    start=(kt == 0), stop=(kt == 31)),
                     reads=[("vsb", p), ("pt", par, kt)], writes=[PS("ct%d" % par)],
                     tag=f"p2.pv{g}.{qc}.{kt}")
            S.op("vector", lambda v, p=p, par=par, qc=qc: v.tensor_tensor(
                out=ctxo[p][:, qc * 512:(qc + 1) * 512], in0=CT[par][:, :],
                in1=lbc[par][:, :], op=ALU.mult),
                 reads=[PS("ct%d" % par), ("lbcs", par)], writes=[("ctxo", p, qc)],
                 tag=f"p2.cev{g}.{qc}")
            emit_conv_ic(8 * g + qc)

        emit_smm(0)
        for qc in range(8):
            if qc + 1 < 8:
                emit_smm(qc + 1)
            emit_tail(qc)
        S.op("scalar", lambda e, g=g, p=p: e.dma_start(
            out=ctx_loc[g * 128:(g + 1) * 128, :], in_=ctxo[p][:, :]),
             reads=[("ctxo", p, qc) for qc in range(8)], writes=[("ctxloc", g)],
             dma=True, tag=f"p2.cst{g}")
        if g == 3:
            S.op("gpsimd", lambda e: e.collective_compute(
                "AllGather", ALU.bypass, replica_groups=[list(range(NCORE))],
                ins=[ctx_loc[0:512, :]], outs=[ctx_gath[0]]),
                 reads=[("ctxloc", gg) for gg in range(4)], writes=[("ctxgath", 0)],
                 cc=True, tag="p2.agA")

    S.op("gpsimd", lambda e: e.collective_compute(
        "AllGather", ALU.bypass, replica_groups=[list(range(NCORE))],
        ins=[ctx_loc[512:1024, :]], outs=[ctx_gath[1]]),
         reads=[("ctxloc", g) for g in range(4, 8)], writes=[("ctxgath", 1)],
         cc=True, tag="p2.agB")

    S.fence()

    for n_ in range(16):
        h = n_ % 2
        S.op("gpsimd", lambda e, n_=n_, h=h: e.indirect_dma_start(
            out=ctxin[:, n_, :], out_offset=None,
            in_=ctx_gview[h], in_offset=IndirectOffsetOnAxis(cgi_sb[:, n_:n_ + 1], 0)),
             reads=[("ctxgath", h), ("cgi",)], writes=[("ctxin", n_)], dma=True,
             tag=f"p3.cld{n_}")

    for oc in range(16):
        bi = oc % 2
        S.op("sync", lambda e, oc=oc, bi=bi: e.dma_start(out=wpf[bi][:, :], in_=wp_st[oc]),
             writes=[("wpf", bi)], dma=True, tag=f"p3.wld{oc}")
        S.op("vector", lambda v, bi=bi: v.tensor_copy(out=wpb[bi][:, :], in_=wpf[bi][:, :]),
             reads=[("wpf", bi)], writes=[("wpb", bi)], tag=f"p3.wcv{oc}")
        for tc in range(4):
            sp = tc % 2
            for hc in range(16):
                S.op("tensor", lambda t, bi=bi, hc=hc, tc=tc, sp=sp: t.matmul(
                    SP[sp][:, :], wpb[bi][:, hc * 128:(hc + 1) * 128],
                    ctxin[:, hc, tc * 512:(tc + 1) * 512],
                    start=(hc == 0), stop=(hc == 15)),
                     reads=[("wpb", bi), ("ctxin", hc)], writes=[PS("sp%d" % sp)],
                     tag=f"p3.pmm{oc}.{tc}.{hc}")
            S.op("vector", lambda v, oc=oc, tc=tc, sp=sp: v.tensor_scalar(
                out=zfm[:, oc, tc * 512:(tc + 1) * 512], in0=SP[sp][:, :],
                scalar1=bp_sb[:, oc:oc + 1], scalar2=None, op0=ALU.add),
                 reads=[PS("sp%d" % sp), ("bp",)], writes=[("zfm", oc, tc)],
                 tag=f"p3.pev{oc}.{tc}")

    S.fence()

    for tt in range(16):
        xp = tt % 2
        S.op("sync", lambda e, tt=tt, xp=xp: e.dma_start(
            out=xres[xp][:, :], in_=xr[tt * 128:(tt + 1) * 128, :]),
             writes=[("xres", xp)], dma=True, tag=f"p3.xld{tt}")
        ATMKEYS = []
        for q in range(4):
            tp = q % 2
            ZK = [("zfm", 4 * q + j, tt // 4) for j in range(4)]
            S.op("vector", lambda v, q=q, tt=tt, tp=tp: v.tensor_copy(
                out=zstg[tp].rearrange("p (j f) -> p j f", j=4),
                in_=zfm[:, 4 * q:4 * q + 4, tt * 128:(tt + 1) * 128]),
                 reads=ZK, writes=[("zstg", tp)], tag=f"p3.zsg{tt}.{q}")
            for j in range(4):
                S.op("tensor", lambda t, j=j, tp=tp: t.transpose(
                    SP[tp][:, j * 128:(j + 1) * 128],
                    zstg[tp][:, j * 128:(j + 1) * 128], ident_f[:, :]),
                     reads=[("zstg", tp), ("identf",)], writes=[PS("sp%d" % tp)],
                     tag=f"p3.ztp{tt}.{q}.{j}")
            S.op("vector", lambda v, q=q, tp=tp: v.tensor_copy(
                out=atm[:, q * 512:(q + 1) * 512], in_=SP[tp][:, :]),
                 reads=[PS("sp%d" % tp)], writes=[("atm", q)], tag=f"p3.zev{tt}.{q}")
            ATMKEYS.append(("atm", q))
        st = stats[tt % 2]
        key = ln_stats(atm[:, :], st, "a" + str(tt), t1b, ATMKEYS, stbuf=tt % 2,
                       sqkey=("t1",))
        S.op("vector", lambda v, st=st: v.tensor_scalar(
            out=t1b[:, :], in0=atm[:, :], scalar1=st[:, 0:1], scalar2=st[:, 2:3],
            op0=ALU.subtract, op1=ALU.mult),
             reads=ATMKEYS + [key + (0,), key + (2,)], writes=[("t1",)],
             tag=f"p3.nrm{tt}")
        S.op("vector", lambda v, xp=xp: v.tensor_tensor(
            out=ybuf[:, :], in0=t1b[:, :], in1=xres[xp][:, :], op=ALU.add),
             reads=[("t1",), ("xres", xp)], writes=[("ybuf",)], tag=f"p3.yadd{tt}")
        S.op("scalar", lambda e, tt=tt: e.dma_start(
            out=y_scr[tt * 128:(tt + 1) * 128, :], in_=ybuf[:, :]),
             reads=[("ybuf",)], writes=[("yscr", tt)], dma=True, tag=f"p3.yst{tt}")
        key2 = ln_stats(ybuf[:, :], st, "y" + str(tt), t1b, [("ybuf",)], stoff=8,
                        stbuf=tt % 2, sqkey=("t1",))
        S.op("vector", lambda v, st=st: v.tensor_scalar(
            out=ln2tm[:, :], in0=ybuf[:, :], scalar1=st[:, 8:9], scalar2=st[:, 10:11],
            op0=ALU.subtract, op1=ALU.mult),
             reads=[("ybuf",), key2 + (0,), key2 + (2,)], writes=[("ln2tm",)],
             tag=f"p3.nrm2{tt}")
        for q in range(4):
            tp = q % 2
            for j in range(4):
                hc = q * 4 + j
                S.op("tensor", lambda t, hc=hc, j=j, tp=tp: t.transpose(
                    CT[tp][:, j * 128:(j + 1) * 128],
                    ln2tm[:, hc * 128:(hc + 1) * 128], ident_f[:, :]),
                     reads=[("ln2tm",), ("identf",)], writes=[PS("ct%d" % tp)],
                     tag=f"p3.l2tp{tt}.{hc}")
            S.op("vector", lambda v, q=q, tt=tt, tp=tp: v.tensor_copy(
                out=ln2fm[:, q * 4:(q + 1) * 4, tt * 128:(tt + 1) * 128],
                in_=CT[tp].rearrange("p (j f) -> p j f", j=4)),
                 reads=[PS("ct%d" % tp)], writes=[("ln2fm", tt)],
                 tag=f"p3.l2ev{tt}.{q}")

    S.fence()

    LN2ALL = [("ln2fm", tt) for tt in range(16)]
    for tp_ in range(4):
        for hc in range(16):
            S.op("tensor", lambda t, hc=hc, tp_=tp_: t.matmul(
                LP[0:1, :], ones_col[:, :], ln2fm[:, hc, tp_ * 512:(tp_ + 1) * 512],
                start=(hc == 0), stop=(hc == 15)),
                 reads=LN2ALL + [("onec",)], writes=[PS("lp")], tag=f"f.sr{tp_}.{hc}")
        S.op("vector", lambda v: v.tensor_copy(out=srow_f[0:1, :], in_=LP[0:1, :]),
             reads=[PS("lp")], writes=[("srowf",)], tag=f"f.src{tp_}")
        S.op("vector", lambda v: v.tensor_copy(out=srow_b[0:1, :], in_=srow_f[0:1, :]),
             reads=[("srowf",)], writes=[("srowb",)], tag=f"f.srb{tp_}")
        for ic in range(64):
            bi = ic % 2
            S.op("sync", lambda e, ic=ic, bi=bi: e.dma_start(
                out=gubuf[bi][:, :, :], in_=gu_scr[ic].rearrange("g p c -> p g c")),
                 reads=[("guscr", ic, 0), ("guscr", ic, 1)], writes=[("gbuf", bi)],
                 dma=True, tag=f"f.gld{tp_}.{ic}")
            for hc in range(16):
                S.op("tensor", lambda t, bi=bi, hc=hc, tp_=tp_: t.matmul(
                    SP[bi][:, :], gubuf[bi][:, 0, hc * 128:(hc + 1) * 128],
                    ln2fm[:, hc, tp_ * 512:(tp_ + 1) * 512],
                    start=(hc == 0), stop=False),
                     reads=[("gbuf", bi)] + LN2ALL, writes=[PS("sp%d" % bi)],
                     tag=f"f.gmm{tp_}.{ic}.{hc}")
            S.op("tensor", lambda t, bi=bi: t.matmul(
                SP[bi][:, :], m128row[:, :], srow_b[0:1, :], start=False, stop=True),
                 reads=[("m128",), ("srowb",)], writes=[PS("sp%d" % bi)],
                 tag=f"f.gcr{tp_}.{ic}")
            for hc in range(16):
                S.op("tensor", lambda t, bi=bi, hc=hc, tp_=tp_: t.matmul(
                    CT[bi][:, :], gubuf[bi][:, 1, hc * 128:(hc + 1) * 128],
                    ln2fm[:, hc, tp_ * 512:(tp_ + 1) * 512],
                    start=(hc == 0), stop=False),
                     reads=[("gbuf", bi)] + LN2ALL, writes=[PS("ct%d" % bi)],
                     tag=f"f.umm{tp_}.{ic}.{hc}")
            S.op("tensor", lambda t, bi=bi: t.matmul(
                CT[bi][:, :], m128row[:, :], srow_b[0:1, :], start=False, stop=True),
                 reads=[("m128",), ("srowb",)], writes=[PS("ct%d" % bi)],
                 tag=f"f.ucr{tp_}.{ic}")
            S.op("vector", lambda v, bi=bi, ic=ic: v.tensor_scalar(
                out=ustg[:, :], in0=CT[bi][:, :], scalar1=us_sb[:, ic:ic + 1],
                scalar2=None, op0=ALU.mult),
                 reads=[PS("ct%d" % bi), ("us",)], writes=[("ustg",)],
                 tag=f"f.uev{tp_}.{ic}")
            S.op("scalar", lambda s, bi=bi, ic=ic: s.activation(
                out=gstg[:, :], in_=SP[bi][:, :], func=AF.Silu,
                bias=0.0, scale=gs_sb[:, ic:ic + 1]),
                 reads=[PS("sp%d" % bi), ("gs",)], writes=[("gstg",)],
                 tag=f"f.gevs{tp_}.{ic}")
            S.op("vector", lambda v, ic=ic: v.tensor_tensor(
                out=h2[:, ic, :], in0=gstg[:, :], in1=ustg[:, :], op=ALU.mult),
                 reads=[("gstg",), ("ustg",)], writes=[("h2", ic)], tag=f"f.h2{tp_}.{ic}")
        accname = ["sp0", "sp1", "sp2", "sp3", "ct0", "ct1", "lp", "bc"]
        for tth in range(2):
            for ic in range(64):
                bi = ic % 2
                S.op("sync", lambda e, ic=ic, bi=bi: e.dma_start(
                    out=dbuf[bi][:, :], in_=dbf_scr[ic]),
                     reads=[("dscr", ic)], writes=[("dbuf", bi)], dma=True,
                     tag=f"f.dld{tp_}.{tth}.{ic}")
                for t3 in range(2):
                    tloc = tth * 2 + t3
                    for hq in range(4):
                        ai = t3 * 4 + hq
                        S.op("tensor", lambda t, ic=ic, bi=bi, tloc=tloc, hq=hq,
                             ai=ai: t.matmul(
                            ACC8[ai][:, :], h2[:, ic, tloc * 128:(tloc + 1) * 128],
                            dbuf[bi][:, hq * 512:(hq + 1) * 512],
                            start=(ic == 0), stop=(ic == 63)),
                             reads=[("h2", ic), ("dbuf", bi)],
                             writes=[PS(accname[ai])],
                             tag=f"f.dmm{tp_}.{tth}.{ic}.{tloc}.{hq}")
            for t3 in range(2):
                for hq in range(4):
                    ai = t3 * 4 + hq
                    S.op("vector", lambda v, t3=t3, hq=hq, ai=ai: v.tensor_copy(
                        out=ftm[t3][:, hq * 512:(hq + 1) * 512], in_=ACC8[ai][:, :]),
                         reads=[PS(accname[ai])], writes=[("ftm", t3, hq)],
                         tag=f"f.dev{tp_}.{tth}.{t3}.{hq}")
            for t3 in range(2):
                gt = tp_ * 4 + tth * 2 + t3
                FKEYS = [("ftm", t3, i) for i in range(4)]
                st = stats[t3]
                key = ln_stats(ftm[t3][:, :], st, f"f{tp_}_{tth}_{t3}", outb, FKEYS,
                               stbuf=t3, sqkey=("outb",))
                S.op("sync", lambda e, gt=gt: e.dma_start(
                    out=yresb[:, :], in_=y_scr[gt * 128:(gt + 1) * 128, :]),
                     reads=[("yscr", gt)], writes=[("yresb",)], dma=True,
                     tag=f"f.yld{gt}")
                S.op("vector", lambda v, t3=t3, st=st: v.tensor_scalar(
                    out=outb[:, :], in0=ftm[t3][:, :], scalar1=st[:, 0:1],
                    scalar2=st[:, 2:3], op0=ALU.subtract, op1=ALU.mult),
                     reads=FKEYS + [key + (0,), key + (2,)], writes=[("outb",)],
                     tag=f"f.nrm{gt}")
                S.op("vector", lambda v: v.tensor_tensor(
                    out=outb[:, :], in0=outb[:, :], in1=yresb[:, :], op=ALU.add),
                     reads=[("outb",), ("yresb",)], writes=[("outb",)],
                     tag=f"f.oadd{gt}")
                S.op("scalar", lambda e, gt=gt: e.dma_start(
                    out=y_out[gt * 128:(gt + 1) * 128, :], in_=outb[:, :]),
                     reads=[("outb",)], writes=[("yout", gt)], dma=True,
                     tag=f"f.ost{gt}")

    S.op("gpsimd", lambda e: e.engine_nop(),
         reads=[("yout", gt) for gt in range(16)], tag="tail")

    with ExitStack() as ctx:
        sems_c = {e: ctx.enter_context(nc.semaphore(f"semc_{e}")) for e in ENGS}
        sems_d = {e: [ctx.enter_context(nc.semaphore(f"semd_{e}{i}"))
                      for i in range(8 if e == "sync" else 4)]
                  for e in ENGS}
        ccsem = [ctx.enter_context(nc.semaphore(f"sem_cc{i}")) for i in range(2)]
        block = ctx.enter_context(nc.Block())
        S.emit(block, sems_c, sems_d, ccsem)
    return nc


def kernel(L0_attn_query, L0_attn_key, L0_attn_value, L0_attn_Wqkv,
           L0_attn_Wproj, L0_attn_bproj, L0_ffn_gate_w, L0_ffn_gate_scale,
           L0_ffn_up_w, L0_ffn_up_scale, L0_ffn_down_w, L0_ffn_down_scale):
    _ = (L0_attn_key, L0_attn_value)
    x = np.ascontiguousarray(np.asarray(L0_attn_query, dtype=np.float32))
    Wqkv = np.asarray(L0_attn_Wqkv, dtype=np.float32)
    Wproj = np.asarray(L0_attn_Wproj, dtype=np.float32)
    bproj = np.asarray(L0_attn_bproj, dtype=np.float32)
    gw = np.asarray(L0_ffn_gate_w, dtype=np.int32)
    uw = np.asarray(L0_ffn_up_w, dtype=np.int32)
    dw = np.asarray(L0_ffn_down_w, dtype=np.int32)
    gsc = np.asarray(L0_ffn_gate_scale, dtype=np.float32)
    usc = np.asarray(L0_ffn_up_scale, dtype=np.float32)
    dsc = np.asarray(L0_ffn_down_scale, dtype=np.float32)

    wq_all = np.ascontiguousarray(
        Wqkv.reshape(16, 3, 128, 16, 128).transpose(0, 1, 4, 3, 2))
    wp_st = np.ascontiguousarray(
        Wproj.reshape(16, 128, 16, 128).transpose(0, 3, 2, 1))
    gw_st = np.ascontiguousarray(
        gw.reshape(64, 128, 16, 128).transpose(0, 3, 2, 1))
    uw_st = np.ascontiguousarray(
        uw.reshape(64, 128, 16, 128).transpose(0, 3, 2, 1))
    dw_st = np.ascontiguousarray(dw.T).reshape(64, 128, 2048)
    gs_st = np.ascontiguousarray(gsc.reshape(64, 128).T)
    us_st = np.ascontiguousarray(usc.reshape(64, 128).T)
    ds_st = np.ascontiguousarray(dsc.reshape(1, 2048))
    bp_st = np.ascontiguousarray(bproj.reshape(16, 128).T)

    in_maps = []
    for c in range(NCORE):
        bg, sh = c // 2, c % 2
        hset = list(range(8 * (c % 2), 8 * (c % 2) + 8))
        cgi = np.empty((128, 16), np.int32)
        for n_ in range(16):
            g = 4 * n_ + bg
            r_src = 2 * (g // 16) + (1 if (g % 16) >= 8 else 0)
            slot_h = (g % 16) % 4
            row0 = r_src * 512 + slot_h * 128
            cgi[:, n_] = (row0 + np.arange(128)) * 2 + sh
        in_maps.append({
            "xs": np.ascontiguousarray(x[bg]),
            "xr": np.ascontiguousarray(x[bg, sh * 2048:(sh + 1) * 2048]),
            "wq_st": np.ascontiguousarray(wq_all[hset]),
            "wp_st": wp_st,
            "gw_st": gw_st,
            "uw_st": uw_st,
            "dw_st": dw_st,
            "gs_in": gs_st,
            "us_in": us_st,
            "ds_in": ds_st,
            "bp_in": bp_st,
            "cg_in": cgi,
        })

    nc = build_program()
    res = run_bass_kernel_spmd(nc, in_maps, core_ids=list(range(NCORE)))
    out = np.empty((B, SQ, H), np.float32)
    for c in range(NCORE):
        bg, sh = c // 2, c % 2
        out[bg, sh * 2048:(sh + 1) * 2048, :] = res.results[c]["y_out"]
    return out


# version 11 (final)
# speedup vs baseline: 1.0413x; 1.0413x over previous
import numpy as np
import concourse.bass as bass
import concourse.mybir as mybir
from concourse.bass_utils import run_bass_kernel_spmd
from concourse.bass_core_rust import IndirectOffsetOnAxis
from contextlib import ExitStack

F32 = mybir.dt.float32
BF16 = mybir.dt.bfloat16
I32 = mybir.dt.int32
AF = mybir.ActivationFunctionType
ALU = mybir.AluOpType
AX = mybir.AxisListType

B, SQ, H = 4, 4096, 2048
NH, HD = 16, 128
NORM = 1.0 / float(np.sqrt(HD))
IFF = 8192
EPS = 1e-5
TPC = 2048
NCORE = 8

ENGS = ("sync", "gpsimd", "vector", "scalar", "tensor")
ARENA_BASE = 17408
ARENA_TOP = 229376


class Sched:

    def __init__(self):
        self.ops = []
        self.last_w = {}
        self.readers = {}
        self.fence_pend = None
        self.fence_seen = set()

    def op(self, eng, fn, reads=(), writes=(), dma=False, cc=False, tag=""):
        idx = len(self.ops)
        deps = set()
        for r in reads:
            if r in self.last_w:
                deps.add((self.last_w[r], "raw"))
        for w in writes:
            if w in self.last_w:
                deps.add((self.last_w[w], "waw"))
            for rd in self.readers.get(w, ()):
                deps.add((rd, "war"))
        if self.fence_pend is not None and eng not in self.fence_seen:
            for j in self.fence_pend:
                deps.add((j, "fence"))
            self.fence_seen.add(eng)
        o = dict(idx=idx, eng=eng, fn=fn, dma=dma, cc=cc, deps=deps,
                 needed=False, tag=tag)
        self.ops.append(o)
        for r in reads:
            self.readers.setdefault(r, []).append(idx)
        for w in writes:
            self.last_w[w] = idx
            self.readers[w] = []
        return idx

    def fence(self):
        last_per_eng = {}
        for o in self.ops:
            last_per_eng[o["eng"]] = o["idx"]
        self.fence_pend = set(last_per_eng.values())
        self.fence_seen = set()

    def _edge_needed(self, p, c, typ):
        if p["cc"] or c["cc"]:
            return True
        if p["eng"] == c["eng"]:
            if typ == "fence":
                return False
            if p["dma"] and c["dma"]:
                return False
            if p["eng"] == "tensor" and not p["dma"] and not c["dma"]:
                return False
            if typ == "war":
                return p["dma"]
            return True
        return True

    def emit(self, block, sems_c, sems_d, ccsem):
        ops = self.ops
        for o in ops:
            fdeps = []
            for (j, typ) in o["deps"]:
                p = ops[j]
                if self._edge_needed(p, o, typ):
                    fdeps.append(j)
                    p["needed"] = True
            o["fdeps"] = fdeps
        cum = {}
        dma_count = {}
        for o in ops:
            if o["dma"]:
                o["needed"] = True
            if o["cc"]:
                nc_cc = cum.get("cc", 0)
                cum["cc"] = nc_cc + 1
                o["semkey"] = ("cc", nc_cc)
                o["semval"] = 1
                o["prewait"] = None
            elif o["dma"]:
                eng = o["eng"]
                n = dma_count.get(eng, 0)
                dma_count[eng] = n + 1
                pool = len(sems_d[eng])
                slot = n % pool
                key = (eng, "dma", slot)
                prev = cum.get(key, 0)
                o["prewait"] = (key, prev) if prev > 0 else None
                cum[key] = prev + 16
                o["semkey"] = key
                o["semval"] = cum[key]
            elif o["needed"]:
                key = (eng_k := o["eng"], "c")
                cum[key] = cum.get(key, 0) + 1
                o["semkey"] = key
                o["semval"] = cum[key]
                o["prewait"] = None
            else:
                o["semkey"] = None
                o["prewait"] = None

        def sem_of(key):
            if key[0] == "cc":
                return ccsem[key[1]]
            if key[1] == "dma":
                return sems_d[key[0]][key[2]]
            return sems_c[key[0]]

        for o in ops:
            waits = {}
            for j in o["fdeps"]:
                p = ops[j]
                k, v = p["semkey"], p["semval"]
                assert k is not None, (o["tag"], p["tag"])
                if k not in waits or waits[k] < v:
                    waits[k] = v
            if o.get("prewait"):
                k, v = o["prewait"]
                if k not in waits or waits[k] < v:
                    waits[k] = v
            o["waits"] = [(sem_of(k), v) for k, v in waits.items()]

        def body(eng_name):
            def run(eng):
                watermark = {}
                for o in ops:
                    if o["eng"] != eng_name:
                        continue
                    for (s, v) in o["waits"]:
                        if watermark.get(id(s), 0) < v:
                            eng.wait_ge(s, v)
                            watermark[id(s)] = v
                    inst = o["fn"](eng)
                    if o["cc"]:
                        inst.then_inc(ccsem[o["semkey"][1]])
                    elif o["needed"]:
                        inst.then_inc(sem_of(o["semkey"]), 16 if o["dma"] else 1)
            return run

        block.sync(body("sync"))
        block.gpsimd(body("gpsimd"))
        block.vector(body("vector"))
        block.scalar(body("scalar"))
        block.tensor(body("tensor"))


def build_program(dbg=False):
    nc = bass.Bass("TRN2", target_bir_lowering=False, debug=False)

    xs = nc.dram_tensor("xs", [SQ, H], F32, kind="ExternalInput").ap()
    xr = nc.dram_tensor("xr", [TPC, H], F32, kind="ExternalInput").ap()
    wq_st = nc.dram_tensor("wq_st", [8, 3, 128, 16, 128], F32, kind="ExternalInput").ap()
    wp_st = nc.dram_tensor("wp_st", [16, 128, 16, 128], F32, kind="ExternalInput").ap()
    gw_st = nc.dram_tensor("gw_st", [64, 128, 16, 128], I32, kind="ExternalInput").ap()
    uw_st = nc.dram_tensor("uw_st", [64, 128, 16, 128], I32, kind="ExternalInput").ap()
    dw_st = nc.dram_tensor("dw_st", [64, 128, 2048], I32, kind="ExternalInput").ap()
    gs_in = nc.dram_tensor("gs_in", [128, 64], F32, kind="ExternalInput").ap()
    us_in = nc.dram_tensor("us_in", [128, 64], F32, kind="ExternalInput").ap()
    ds_in = nc.dram_tensor("ds_in", [1, 2048], F32, kind="ExternalInput").ap()
    bp_in = nc.dram_tensor("bp_in", [128, 16], F32, kind="ExternalInput").ap()
    cg_in = nc.dram_tensor("cg_in", [128, 16], I32, kind="ExternalInput").ap()
    y_out = nc.dram_tensor("y_out", [TPC, H], F32, kind="ExternalOutput").ap()

    KD = "ExternalOutput" if dbg else "Internal"
    q_scr = nc.dram_tensor("q_scr", [8, 128, SQ], BF16, kind=KD).ap()
    k_scr = nc.dram_tensor("k_scr", [8, 128, SQ], BF16, kind=KD).ap()
    v_scr = nc.dram_tensor("v_scr", [8, 128, 32, 128], BF16, kind=KD).ap()
    gu_scr = nc.dram_tensor("gu_scr", [64, 2, 128, 2048], BF16, kind="Internal").ap()
    dbf_scr = nc.dram_tensor("dbf_scr", [64, 128, 2048], BF16, kind="Internal").ap()
    y_scr = nc.dram_tensor("y_scr", [TPC, H], F32, kind=KD).ap()
    ctx_loc = nc.dram_tensor("ctx_loc", [8 * 128, SQ], BF16, kind=KD).ap()
    ctx_gath = [nc.dram_tensor(f"ctx_gath{h}", [NCORE * 4 * 128, SQ], BF16,
                               kind="Internal", addr_space="Shared").ap()
                for h in range(2)]
    ctx_gview = [g.rearrange("r (h c) -> (r h) c", h=2) for g in ctx_gath]

    cur = [ARENA_BASE]

    def alloc(name, shape, dt, at=None):
        if at is None:
            at = cur[0]
        szmap = {F32: 4, BF16: 2, I32: 4}
        free_elems = int(np.prod(shape[1:])) if len(shape) > 1 else shape[0]
        nbytes = free_elems * szmap[dt]
        t = nc.alloc_sbuf_tensor_at(name, list(shape), dt, offset=at).ap()
        end = at + ((nbytes + 31) // 32) * 32
        cur[0] = max(cur[0], end)
        assert end <= ARENA_TOP, (name, at, end)
        return t

    ident = alloc("ident", [128, 128], BF16)
    ident_f = alloc("ident_f", [128, 128], F32)
    ones_col = alloc("ones_col", [128, 1], BF16)
    ones_row = alloc("ones_row", [1, 128], BF16)
    ones_row_f = alloc("ones_row_f", [1, 128], F32)
    m128row = alloc("m128row", [1, 128], BF16)
    gs_sb = alloc("gs_sb", [128, 64], F32)
    us_sb = alloc("us_sb", [128, 64], F32)
    bp_sb = alloc("bp_sb", [128, 16], F32)
    cgi_sb = alloc("cgi_sb", [128, 16], I32)
    ds_row = alloc("ds_row", [1, 2048], F32)
    dsb = alloc("dsb", [128, 2048], F32)
    stats = [alloc(f"stat{i}", [128, 16], F32) for i in range(2)]
    srow_f = alloc("srow_f", [1, 512], F32)
    srow_b = alloc("srow_b", [1, 512], BF16)
    lrow_f = alloc("lrow_f", [1, 512], F32)
    linv_f = alloc("linv_f", [1, 512], F32)
    linv_b = alloc("linv_b", [1, 512], BF16)
    gstat = alloc("gstat", [1, 16], F32)
    gstat_b = alloc("gstat_b", [1, 16], BF16)
    negc = alloc("negc", [128, 2], F32)
    eps_sb = alloc("eps_sb", [128, 1], F32)
    e20_sb = alloc("e20_sb", [128, 1], F32)
    PH = ((cur[0] + 127) // 128) * 128

    ln1fm = alloc("ln1fm", [128, 16, SQ], BF16, at=PH)
    P1A = PH + 131072
    xbuf = [alloc(f"xbuf{i}", [128, 2048], F32, at=P1A + i * 8192) for i in range(2)]
    ln1tm = [alloc(f"ln1tm{i}", [128, 2048], F32, at=P1A + 16384 + i * 8192) for i in range(2)]
    wqf = [alloc(f"wqf{i}", [128, 2048], F32, at=P1A + i * 8192) for i in range(2)]
    wqb = [alloc(f"wqb{i}", [128, 2048], BF16, at=P1A + 16384 + i * 4096) for i in range(2)]
    qkvb = [alloc(f"qkvb{i}", [128, SQ], BF16, at=P1A + 24576 + i * 8192) for i in range(2)]
    vtmb = alloc("vtmb", [128, 32, 128], BF16, at=P1A + 40960)
    vstg = [alloc(f"vstg{i}", [128, 512], F32, at=P1A + 49152 + i * 2048) for i in range(2)]

    ptile = [alloc(f"ptile{i}", [128, 32, 512], BF16, at=PH + i * 32768) for i in range(2)]
    qsb = [alloc(f"qsb{i}", [128, SQ], BF16, at=PH + 65536 + i * 8192) for i in range(2)]
    ksb = [alloc(f"ksb{i}", [128, SQ], BF16, at=PH + 81920 + i * 8192) for i in range(2)]
    vsb = [alloc(f"vsb{i}", [128, 32, 128], BF16, at=PH + 98304 + i * 8192) for i in range(2)]
    q2b = alloc("q2b", [128, SQ], BF16, at=PH + 114688)
    ctxo = [alloc(f"ctxo{i}", [128, SQ], BF16, at=PH + 122880 + i * 8192) for i in range(2)]
    lbc = [alloc(f"lbc{i}", [128, 512], BF16, at=PH + 139264 + i * 1024) for i in range(2)]
    cvin = [alloc(f"cvin{i}", [128, 2048], I32, at=PH + 141312 + i * 8192) for i in range(2)]
    cvbf = [alloc(f"cvbf{i}", [128, 2048], BF16, at=PH + 157696 + i * 4096) for i in range(2)]
    cvtmp = alloc("cvtmp", [128, 2048], F32, at=PH + 165888)

    ctxin = alloc("ctxin", [128, 16, 2048], BF16, at=PH)
    zfm = alloc("zfm", [128, 16, 2048], BF16, at=PH + 65536)
    wpf = [alloc(f"wpf{i}", [128, 2048], F32, at=PH + 131072 + i * 8192) for i in range(2)]
    wpb = [alloc(f"wpb{i}", [128, 2048], BF16, at=PH + 147456 + i * 4096) for i in range(2)]
    ln2fm = alloc("ln2fm", [128, 16, 2048], BF16, at=PH)
    t1b = alloc("t1b", [128, 2048], F32, at=PH + 131072)
    ln2tm = alloc("ln2tm", [128, 2048], F32, at=PH + 139264)
    atm = alloc("atm", [128, 2048], BF16, at=PH + 147456)
    zstg = [alloc(f"zstg{i}", [128, 512], F32, at=PH + 151552 + i * 2048) for i in range(2)]
    xres = [alloc(f"xres{i}", [128, 2048], F32, at=PH + 155648 + i * 8192) for i in range(2)]
    ybuf = alloc("ybuf", [128, 2048], F32, at=PH + 172032)

    h2 = alloc("h2", [128, 64, 512], BF16, at=PH + 65536)
    gubuf = [alloc(f"gubuf{i}", [128, 2, 2048], BF16, at=PH + 131072 + i * 8192) for i in range(2)]
    dbuf = [alloc(f"dbuf{i}", [128, 2048], BF16, at=PH + 147456 + i * 4096) for i in range(2)]
    gstg = alloc("gstg", [128, 512], F32, at=PH + 155648)
    ustg = alloc("ustg", [128, 512], F32, at=PH + 157696)
    ftm = [alloc(f"ftm{i}", [128, 2048], BF16, at=PH + 159744 + i * 4096) for i in range(2)]
    yresb = alloc("yresb", [128, 2048], F32, at=PH + 167936)
    outb = alloc("outb", [128, 2048], F32, at=PH + 176128)

    SPD = [nc.alloc_psum_tensor(f"spd{i}", [128, 1024], F32).ap() for i in range(2)]
    CT = [nc.alloc_psum_tensor(f"ct{i}", [128, 512], F32).ap() for i in range(2)]
    LP = nc.alloc_psum_tensor("lp", [128, 512], F32).ap()
    BCp = nc.alloc_psum_tensor("bc", [128, 512], F32).ap()
    SP = [SPD[0][:, 0:512], SPD[0][:, 512:1024],
          SPD[1][:, 0:512], SPD[1][:, 512:1024]]
    ACC8 = SP + CT + [LP, BCp]

    S = Sched()
    PS = lambda name: ("ps", name)

    def ln_stats(x_ap, st, pfx, sqdump, inkeys, stoff=0, n=2048, stbuf=None,
                 sqkey=("sqd",)):
        key = ("st", stbuf, stoff)
        c = lambda i: slice(stoff + i, stoff + i + 1)
        S.op("vector", lambda v, a=x_ap, o=st: v.reduce_sum(o[:, c(0)], a, axis=AX.X),
             reads=inkeys, writes=[key + (0,)], tag=pfx + ".sum")
        S.op("scalar", lambda s, a=x_ap, o=st, d=sqdump: s.activation(
            out=d[:, :], in_=a, func=AF.Square, accum_out=o[:, c(1)]),
             reads=inkeys, writes=[key + (1,), sqkey], tag=pfx + ".sq")
        S.op("vector", lambda v, o=st: v.tensor_scalar(
            out=o[:, c(0)], in0=o[:, c(0)], scalar1=1.0 / n, scalar2=None, op0=ALU.mult),
             reads=[key + (0,)], writes=[key + (0,)], tag=pfx + ".mean")
        S.op("vector", lambda v, o=st: v.tensor_scalar(
            out=o[:, c(1)], in0=o[:, c(1)], scalar1=1.0 / n, scalar2=None, op0=ALU.mult),
             reads=[key + (1,)], writes=[key + (1,)], tag=pfx + ".msq")
        S.op("vector", lambda v, o=st: v.tensor_tensor(
            out=o[:, c(3)], in0=o[:, c(0)], in1=o[:, c(0)], op=ALU.mult),
             reads=[key + (0,)], writes=[key + (3,)], tag=pfx + ".m2")
        S.op("vector", lambda v, o=st: v.tensor_tensor(
            out=o[:, c(1)], in0=o[:, c(1)], in1=o[:, c(3)], op=ALU.subtract),
             reads=[key + (1,), key + (3,)], writes=[key + (1,)], tag=pfx + ".var")
        S.op("scalar", lambda s, o=st: s.activation(
            out=o[:, c(2)], in_=o[:, c(1)], func=AF.Ln, bias=eps_sb[:, 0:1], scale=1.0),
             reads=[key + (1,), ("epsb",)], writes=[key + (2,)], tag=pfx + ".ln")
        S.op("scalar", lambda s, o=st: s.activation(
            out=o[:, c(2)], in_=o[:, c(2)], func=AF.Exp, bias=0.0, scale=-0.5),
             reads=[key + (2,)], writes=[key + (2,)], tag=pfx + ".exp")
        return key

    S.op("gpsimd", lambda g: g.memset(ident[:, :], 0.0), writes=[("id0",)], tag="id0")
    S.op("gpsimd", lambda g: g.affine_select(
        out=ident[:, :], in_=ident[:, :], compare_op=ALU.not_equal, fill=1.0,
        base=0, pattern=[[-1, 128]], channel_multiplier=1),
         reads=[("id0",)], writes=[("ident",)], tag="id1")
    S.op("gpsimd", lambda g: g.memset(ident_f[:, :], 0.0), writes=[("idf0",)], tag="idf0")
    S.op("gpsimd", lambda g: g.affine_select(
        out=ident_f[:, :], in_=ident_f[:, :], compare_op=ALU.not_equal, fill=1.0,
        base=0, pattern=[[-1, 128]], channel_multiplier=1),
         reads=[("idf0",)], writes=[("identf",)], tag="idf1")
    S.op("gpsimd", lambda g: g.memset(ones_col[:, :], 1.0), writes=[("onec",)], tag="onec")
    S.op("gpsimd", lambda g: g.memset(ones_row[:, :], 1.0), writes=[("oner",)], tag="oner")
    S.op("gpsimd", lambda g: g.memset(ones_row_f[:, :], 1.0), writes=[("onerf",)], tag="onerf")
    S.op("gpsimd", lambda g: g.memset(m128row[:, :], -128.0), writes=[("m128",)], tag="m128")
    S.op("gpsimd", lambda g: g.memset(eps_sb[:, :], EPS), writes=[("epsb",)], tag="epsb")
    S.op("gpsimd", lambda g: g.memset(e20_sb[:, :], 1e-20), writes=[("e20b",)], tag="e20b")
    S.op("sync", lambda e: e.dma_start(out=gs_sb[:, :], in_=gs_in), writes=[("gs",)], dma=True, tag="ld.gs")
    S.op("sync", lambda e: e.dma_start(out=us_sb[:, :], in_=us_in), writes=[("us",)], dma=True, tag="ld.us")
    S.op("sync", lambda e: e.dma_start(out=bp_sb[:, :], in_=bp_in), writes=[("bp",)], dma=True, tag="ld.bp")
    S.op("sync", lambda e: e.dma_start(out=cgi_sb[:, :], in_=cg_in), writes=[("cgi",)], dma=True, tag="ld.cg")
    S.op("sync", lambda e: e.dma_start(out=ds_row[:, :], in_=ds_in), writes=[("dsr",)], dma=True, tag="ld.ds")
    for j in range(4):
        S.op("tensor", lambda t, j=j: t.matmul(
            BCp[:, :], ones_row_f[:, :], ds_row[0:1, j * 512:(j + 1) * 512],
            start=True, stop=True),
             reads=[("onerf",), ("dsr",)], writes=[PS("bc")], tag=f"dsb.mm{j}")
        S.op("vector", lambda v, j=j: v.tensor_copy(
            out=dsb[:, j * 512:(j + 1) * 512], in_=BCp[:, :]),
             reads=[PS("bc")], writes=[("dsb",)], tag=f"dsb.cp{j}")

    for tt in range(32):
        p = tt % 2
        S.op("sync", lambda e, tt=tt, p=p: e.dma_start(
            out=xbuf[p][:, :], in_=xs[tt * 128:(tt + 1) * 128, :]),
             writes=[("xb", p)], dma=True, tag=f"p1.x{tt}")
        pfx = "ln1_" + str(tt)
        key = ln_stats(xbuf[p][:, :], stats[p], pfx, ln1tm[p], [("xb", p)], stbuf=p,
                       sqkey=("ln1tm", p))
        S.op("vector", lambda v, p=p: v.tensor_scalar(
            out=ln1tm[p][:, :], in0=xbuf[p][:, :],
            scalar1=stats[p][:, 0:1], scalar2=stats[p][:, 2:3],
            op0=ALU.subtract, op1=ALU.mult),
             reads=[("xb", p), key + (0,), key + (2,)],
             writes=[("ln1tm", p)], tag=f"p1.nrm{tt}")
        for q in range(4):
            tp = q % 2
            for j in range(4):
                hc = q * 4 + j
                S.op("tensor", lambda t, p=p, hc=hc, j=j, tp=tp: t.transpose(
                    SP[tp][:, j * 128:(j + 1) * 128],
                    ln1tm[p][:, hc * 128:(hc + 1) * 128], ident_f[:, :]),
                     reads=[("ln1tm", p), ("identf",)], writes=[PS("sp%d" % tp)],
                     tag=f"p1.tp{tt}.{hc}")
            S.op("vector", lambda v, p=p, q=q, tt=tt, tp=tp: v.tensor_copy(
                out=ln1fm[:, q * 4:(q + 1) * 4, tt * 128:(tt + 1) * 128],
                in_=SP[tp].rearrange("p (j f) -> p j f", j=4)),
                 reads=[PS("sp%d" % tp)], writes=[("ln1fm", tt)],
                 tag=f"p1.ev{tt}.{q}")

    S.fence()

    LN1ALL = [("ln1fm", tt) for tt in range(32)]
    for hl in range(8):
        for kind in range(3):
            bi = (hl * 3 + kind) % 2
            S.op("sync", lambda e, hl=hl, kind=kind, bi=bi: e.dma_start(
                out=wqf[bi][:, :], in_=wq_st[hl, kind]),
                 writes=[("wqf", bi)], dma=True, tag=f"p1.wld{hl}.{kind}")
            S.op("vector", lambda v, bi=bi: v.tensor_copy(
                out=wqb[bi][:, :], in_=wqf[bi][:, :]),
                 reads=[("wqf", bi)], writes=[("wqb", bi)], tag=f"p1.wcv{hl}.{kind}")
            ob = bi
            for tc in range(8):
                sp = tc % 2
                for hc in range(16):
                    S.op("tensor", lambda t, bi=bi, hc=hc, tc=tc, sp=sp: t.matmul(
                        SP[sp][:, :], wqb[bi][:, hc * 128:(hc + 1) * 128],
                        ln1fm[:, hc, tc * 512:(tc + 1) * 512],
                        start=(hc == 0), stop=(hc == 15)),
                         reads=[("wqb", bi)] + LN1ALL, writes=[PS("sp%d" % sp)],
                         tag=f"p1.mm{hl}.{kind}.{tc}.{hc}")
                S.op("vector", lambda v, ob=ob, tc=tc, sp=sp: v.tensor_copy(
                    out=qkvb[ob][:, tc * 512:(tc + 1) * 512], in_=SP[sp][:, :]),
                     reads=[PS("sp%d" % sp)], writes=[("qkvb", ob, tc)],
                     tag=f"p1.qev{hl}.{kind}.{tc}")
            QK_ALL = [("qkvb", ob, tc) for tc in range(8)]
            if kind == 0:
                S.op("scalar", lambda e, hl=hl, ob=ob: e.dma_start(
                    out=q_scr[hl], in_=qkvb[ob][:, :]),
                     reads=QK_ALL, writes=[("qscr", hl)], dma=True, tag=f"p1.qst{hl}")
            elif kind == 1:
                S.op("scalar", lambda e, hl=hl, ob=ob: e.dma_start(
                    out=k_scr[hl], in_=qkvb[ob][:, :]),
                     reads=QK_ALL, writes=[("kscr", hl)], dma=True, tag=f"p1.kst{hl}")
            else:
                for q in range(8):
                    tp = q % 2
                    S.op("vector", lambda v, q=q, tp=tp, ob=ob: v.tensor_copy(
                        out=vstg[tp][:, :], in_=qkvb[ob][:, q * 512:(q + 1) * 512]),
                         reads=[("qkvb", ob, q)], writes=[("vstg", tp)],
                         tag=f"p1.vsg{hl}.{q}")
                    for j in range(4):
                        S.op("tensor", lambda t, j=j, tp=tp: t.transpose(
                            SP[2 + tp][:, j * 128:(j + 1) * 128],
                            vstg[tp][:, j * 128:(j + 1) * 128], ident_f[:, :]),
                             reads=[("vstg", tp), ("identf",)],
                             writes=[PS("sp%d" % (2 + tp))], tag=f"p1.vtp{hl}.{q}.{j}")
                    S.op("vector", lambda v, q=q, tp=tp: v.tensor_copy(
                        out=vtmb[:, q * 4:(q + 1) * 4, :],
                        in_=SP[2 + tp].rearrange("p (j f) -> p j f", j=4)),
                         reads=[PS("sp%d" % (2 + tp))], writes=[("vtmb", q)],
                         tag=f"p1.vev{hl}.{q}")
                S.op("scalar", lambda e, hl=hl: e.dma_start(
                    out=v_scr[hl], in_=vtmb[:, :, :]),
                     reads=[("vtmb", q) for q in range(8)],
                     writes=[("vscr", hl)], dma=True, tag=f"p1.vst{hl}")

    S.fence()

    def emit_group_load(g):
        p = g % 2
        S.op("sync", lambda e, g=g, p=p: e.dma_start(out=qsb[p][:, :], in_=q_scr[g]),
             reads=[("qscr", g)], writes=[("qsb", p)], dma=True, tag=f"p2.qld{g}")
        S.op("sync", lambda e, g=g, p=p: e.dma_start(out=ksb[p][:, :], in_=k_scr[g]),
             reads=[("kscr", g)], writes=[("ksb", p)], dma=True, tag=f"p2.kld{g}")
        S.op("sync", lambda e, g=g, p=p: e.dma_start(out=vsb[p][:, :, :], in_=v_scr[g]),
             reads=[("vscr", g)], writes=[("vsb", p)], dma=True, tag=f"p2.vld{g}")

    def emit_norm_bound(g, p, src_key, sq_tag, out_col, mkey):
        buf = qsb[p] if src_key == "q" else ksb[p]
        S.op("scalar", lambda s, buf=buf: s.activation(
            out=q2b[:, :], in_=buf[:, :], func=AF.Square),
             reads=[(src_key + "sb", p)], writes=[("q2b",)], tag=f"p2.sq_{sq_tag}{g}")
        for j in range(8):
            S.op("tensor", lambda t, j=j: t.matmul(
                LP[0:1, :], ones_col[:, :], q2b[:, j * 512:(j + 1) * 512],
                start=True, stop=True),
                 reads=[("q2b",), ("onec",)], writes=[PS("lp")], tag=f"p2.nn_{sq_tag}{g}.{j}")
            S.op("vector", lambda v: v.reduce_max(
                gstat[0:1, 8:9], LP[0:1, :], axis=AX.X),
                 reads=[PS("lp")], writes=[("gred",)], tag=f"p2.nm_{sq_tag}{g}.{j}")
            if j == 0:
                S.op("vector", lambda v, oc=out_col: v.tensor_copy(
                    out=gstat[0:1, oc:oc + 1], in_=gstat[0:1, 8:9]),
                     reads=[("gred",)], writes=[mkey], tag=f"p2.n0_{sq_tag}{g}")
            else:
                S.op("vector", lambda v, oc=out_col: v.tensor_tensor(
                    out=gstat[0:1, oc:oc + 1], in0=gstat[0:1, oc:oc + 1],
                    in1=gstat[0:1, 8:9], op=ALU.max),
                     reads=[("gred",), mkey], writes=[mkey], tag=f"p2.nx_{sq_tag}{g}.{j}")

    def emit_conv_ic(ic):
        for kindc, src_t, slot in (("g", gw_st, 0), ("u", uw_st, 1)):
            p = (2 * ic + slot) % 2
            S.op("gpsimd", lambda e, src_t=src_t, ic=ic, p=p: e.dma_start(
                out=cvin[p][:, :], in_=src_t[ic]),
                 writes=[("cvin", p)], dma=True, tag=f"cv.{kindc}l{ic}")
            S.op("vector", lambda v, p=p: v.tensor_copy(
                out=cvbf[p][:, :], in_=cvin[p][:, :]),
                 reads=[("cvin", p)], writes=[("cvbf", p)], tag=f"cv.{kindc}c{ic}")
            S.op("gpsimd", lambda e, ic=ic, p=p, slot=slot: e.dma_start(
                out=gu_scr[ic, slot], in_=cvbf[p][:, :]),
                 reads=[("cvbf", p)], writes=[("guscr", ic, slot)], dma=True,
                 tag=f"cv.{kindc}s{ic}")
        p = ic % 2
        S.op("gpsimd", lambda e, ic=ic, p=p: e.dma_start(
            out=cvin[p][:, :], in_=dw_st[ic]),
             writes=[("cvin", p)], dma=True, tag=f"cv.dl{ic}")
        S.op("vector", lambda v, p=p: v.tensor_scalar(
            out=cvtmp[:, :], in0=cvin[p][:, :], scalar1=128.0, scalar2=None,
            op0=ALU.subtract),
             reads=[("cvin", p)], writes=[("cvtmp",)], tag=f"cv.dsub{ic}")
        S.op("vector", lambda v, p=p: v.tensor_tensor(
            out=cvbf[p][:, :], in0=cvtmp[:, :], in1=dsb[:, :], op=ALU.mult),
             reads=[("cvtmp",), ("dsb",)], writes=[("cvbf", p)], tag=f"cv.dd{ic}")
        S.op("gpsimd", lambda e, ic=ic, p=p: e.dma_start(
            out=dbf_scr[ic], in_=cvbf[p][:, :]),
             reads=[("cvbf", p)], writes=[("dscr", ic)], dma=True,
             tag=f"cv.ds{ic}")

    emit_group_load(0)
    for g in range(8):
        p = g % 2
        if g + 1 < 8:
            emit_group_load(g + 1)
        emit_norm_bound(g, p, "q", "q", 0, ("m2q",))
        emit_norm_bound(g, p, "k", "k", 1, ("m2k",))
        S.op("vector", lambda v: v.tensor_tensor(
            out=gstat[0:1, 4:5], in0=gstat[0:1, 0:1], in1=gstat[0:1, 1:2], op=ALU.mult),
             reads=[("m2q",), ("m2k",)], writes=[("mprod",)], tag=f"p2.cp{g}")
        S.op("scalar", lambda s: s.activation(
            out=gstat[0:1, 4:5], in_=gstat[0:1, 4:5], func=AF.Ln, bias=e20_sb[0:1, 0:1], scale=1.0),
             reads=[("mprod",), ("e20b",)], writes=[("mprod",)], tag=f"p2.cln{g}")
        S.op("scalar", lambda s: s.activation(
            out=gstat[0:1, 4:5], in_=gstat[0:1, 4:5], func=AF.Exp, bias=0.0, scale=0.5),
             reads=[("mprod",)], writes=[("mprod",)], tag=f"p2.cex{g}")
        S.op("vector", lambda v: v.tensor_scalar(
            out=gstat_b[0:1, 0:1], in0=gstat[0:1, 4:5], scalar1=-NORM, scalar2=None,
            op0=ALU.mult),
             reads=[("mprod",)], writes=[("negc1",)], tag=f"p2.cng{g}")
        S.op("tensor", lambda t: t.matmul(
            BCp[:, 0:1], ones_row[:, :], gstat_b[0:1, 0:1], start=True, stop=True),
             reads=[("negc1",), ("oner",)], writes=[PS("bc")], tag=f"p2.cbc{g}")
        S.op("vector", lambda v, p=p: v.tensor_copy(out=negc[:, p:p + 1], in_=BCp[:, 0:1]),
             reads=[PS("bc")], writes=[("negc", p)], tag=f"p2.ccp{g}")

        def emit_smm(qc):
            par = qc % 2
            for kt in range(32):
                sp = kt % 4
                S.op("tensor", lambda t, p=p, kt=kt, qc=qc, sp=sp: t.matmul(
                    SP[sp][:, :], ksb[p][:, kt * 128:(kt + 1) * 128],
                    qsb[p][:, qc * 512:(qc + 1) * 512], start=True, stop=True),
                     reads=[("ksb", p), ("qsb", p)], writes=[PS("sp%d" % sp)],
                     tag=f"p2.smm{g}.{qc}.{kt}")
                if kt % 2 == 1:
                    di = (kt // 2) % 2
                    S.op("scalar", lambda s, par=par, kt=kt, di=di, p=p: s.activation(
                        out=ptile[par].rearrange("p a b -> p (a b)")[
                            :, (kt - 1) * 512:(kt + 1) * 512],
                        in_=SPD[di][:, :], func=AF.Exp,
                        bias=negc[:, p:p + 1], scale=NORM),
                         reads=[PS("sp%d" % (2 * di)), PS("sp%d" % (2 * di + 1)),
                                ("negc", p)],
                         writes=[("pt", par, kt - 1), ("pt", par, kt)],
                         tag=f"p2.exp{g}.{qc}.{kt}")

        def emit_tail(qc):
            par = qc % 2
            for kt in range(32):
                S.op("tensor", lambda t, par=par, kt=kt: t.matmul(
                    LP[0:1, :], ones_col[:, :], ptile[par][:, kt, :],
                    start=(kt == 0), stop=(kt == 31)),
                     reads=[("pt", par, kt), ("onec",)], writes=[PS("lp")],
                     tag=f"p2.lmm{g}.{qc}.{kt}")
            S.op("vector", lambda v: v.reciprocal(out=linv_f[0:1, :], in_=LP[0:1, :]),
                 reads=[PS("lp")], writes=[("linvf",)], tag=f"p2.lrc{g}.{qc}")
            S.op("vector", lambda v: v.tensor_copy(out=linv_b[0:1, :], in_=linv_f[0:1, :]),
                 reads=[("linvf",)], writes=[("linv",)], tag=f"p2.lrb{g}.{qc}")
            S.op("tensor", lambda t: t.matmul(
                BCp[:, :], ones_row[:, :], linv_b[0:1, :], start=True, stop=True),
                 reads=[("linv",), ("oner",)], writes=[PS("bc")], tag=f"p2.lbm{g}.{qc}")
            S.op("vector", lambda v, par=par: v.tensor_copy(out=lbc[par][:, :], in_=BCp[:, :]),
                 reads=[PS("bc")], writes=[("lbcs", par)], tag=f"p2.lbe{g}.{qc}")
            for kt in range(32):
                S.op("tensor", lambda t, p=p, par=par, kt=kt: t.matmul(
                    CT[par][:, :], vsb[p][:, kt, :], ptile[par][:, kt, :],
                    start=(kt == 0), stop=(kt == 31)),
                     reads=[("vsb", p), ("pt", par, kt)], writes=[PS("ct%d" % par)],
                     tag=f"p2.pv{g}.{qc}.{kt}")
            S.op("vector", lambda v, p=p, par=par, qc=qc: v.tensor_tensor(
                out=ctxo[p][:, qc * 512:(qc + 1) * 512], in0=CT[par][:, :],
                in1=lbc[par][:, :], op=ALU.mult),
                 reads=[PS("ct%d" % par), ("lbcs", par)], writes=[("ctxo", p, qc)],
                 tag=f"p2.cev{g}.{qc}")
            emit_conv_ic(8 * g + qc)

        def emit_pair(qc):
            par = qc % 2
            parn = (qc + 1) % 2
            nxt = qc + 1 < 8
            for kt in range(32):
                if nxt:
                    sp = kt % 4
                    S.op("tensor", lambda t, p=p, kt=kt, qc=qc, sp=sp: t.matmul(
                        SP[sp][:, :], ksb[p][:, kt * 128:(kt + 1) * 128],
                        qsb[p][:, (qc + 1) * 512:(qc + 2) * 512], start=True, stop=True),
                         reads=[("ksb", p), ("qsb", p)], writes=[PS("sp%d" % sp)],
                         tag=f"p2.smm{g}.{qc + 1}.{kt}")
                    if kt % 2 == 1:
                        di = (kt // 2) % 2
                        S.op("scalar", lambda s, parn=parn, kt=kt, di=di, p=p: s.activation(
                            out=ptile[parn].rearrange("p a b -> p (a b)")[
                                :, (kt - 1) * 512:(kt + 1) * 512],
                            in_=SPD[di][:, :], func=AF.Exp,
                            bias=negc[:, p:p + 1], scale=NORM),
                             reads=[PS("sp%d" % (2 * di)), PS("sp%d" % (2 * di + 1)),
                                    ("negc", p)],
                             writes=[("pt", parn, kt - 1), ("pt", parn, kt)],
                             tag=f"p2.exp{g}.{qc + 1}.{kt}")
                S.op("tensor", lambda t, par=par, kt=kt: t.matmul(
                    LP[0:1, :], ones_col[:, :], ptile[par][:, kt, :],
                    start=(kt == 0), stop=(kt == 31)),
                     reads=[("pt", par, kt), ("onec",)], writes=[PS("lp")],
                     tag=f"p2.lmm{g}.{qc}.{kt}")
                S.op("tensor", lambda t, p=p, par=par, kt=kt: t.matmul(
                    CT[par][:, :], vsb[p][:, kt, :], ptile[par][:, kt, :],
                    start=(kt == 0), stop=(kt == 31)),
                     reads=[("vsb", p), ("pt", par, kt)], writes=[PS("ct%d" % par)],
                     tag=f"p2.pv{g}.{qc}.{kt}")
            S.op("vector", lambda v: v.reciprocal(out=linv_f[0:1, :], in_=LP[0:1, :]),
                 reads=[PS("lp")], writes=[("linvf",)], tag=f"p2.lrc{g}.{qc}")
            S.op("vector", lambda v: v.tensor_copy(out=linv_b[0:1, :], in_=linv_f[0:1, :]),
                 reads=[("linvf",)], writes=[("linv",)], tag=f"p2.lrb{g}.{qc}")
            S.op("tensor", lambda t: t.matmul(
                BCp[:, :], ones_row[:, :], linv_b[0:1, :], start=True, stop=True),
                 reads=[("linv",), ("oner",)], writes=[PS("bc")], tag=f"p2.lbm{g}.{qc}")
            S.op("vector", lambda v, par=par: v.tensor_copy(out=lbc[par][:, :], in_=BCp[:, :]),
                 reads=[PS("bc")], writes=[("lbcs", par)], tag=f"p2.lbe{g}.{qc}")
            S.op("vector", lambda v, p=p, par=par, qc=qc: v.tensor_tensor(
                out=ctxo[p][:, qc * 512:(qc + 1) * 512], in0=CT[par][:, :],
                in1=lbc[par][:, :], op=ALU.mult),
                 reads=[PS("ct%d" % par), ("lbcs", par)], writes=[("ctxo", p, qc)],
                 tag=f"p2.cev{g}.{qc}")
            emit_conv_ic(8 * g + qc)

        emit_smm(0)
        for qc in range(8):
            emit_pair(qc)
        S.op("scalar", lambda e, g=g, p=p: e.dma_start(
            out=ctx_loc[g * 128:(g + 1) * 128, :], in_=ctxo[p][:, :]),
             reads=[("ctxo", p, qc) for qc in range(8)], writes=[("ctxloc", g)],
             dma=True, tag=f"p2.cst{g}")
        if g == 3:
            S.op("gpsimd", lambda e: e.collective_compute(
                "AllGather", ALU.bypass, replica_groups=[list(range(NCORE))],
                ins=[ctx_loc[0:512, :]], outs=[ctx_gath[0]]),
                 reads=[("ctxloc", gg) for gg in range(4)], writes=[("ctxgath", 0)],
                 cc=True, tag="p2.agA")

    S.op("gpsimd", lambda e: e.collective_compute(
        "AllGather", ALU.bypass, replica_groups=[list(range(NCORE))],
        ins=[ctx_loc[512:1024, :]], outs=[ctx_gath[1]]),
         reads=[("ctxloc", g) for g in range(4, 8)], writes=[("ctxgath", 1)],
         cc=True, tag="p2.agB")

    S.fence()

    for n_ in range(16):
        h = n_ % 2
        S.op("gpsimd", lambda e, n_=n_, h=h: e.indirect_dma_start(
            out=ctxin[:, n_, :], out_offset=None,
            in_=ctx_gview[h], in_offset=IndirectOffsetOnAxis(cgi_sb[:, n_:n_ + 1], 0)),
             reads=[("ctxgath", h), ("cgi",)], writes=[("ctxin", n_)], dma=True,
             tag=f"p3.cld{n_}")

    for oc in range(16):
        bi = oc % 2
        S.op("sync", lambda e, oc=oc, bi=bi: e.dma_start(out=wpf[bi][:, :], in_=wp_st[oc]),
             writes=[("wpf", bi)], dma=True, tag=f"p3.wld{oc}")
        S.op("vector", lambda v, bi=bi: v.tensor_copy(out=wpb[bi][:, :], in_=wpf[bi][:, :]),
             reads=[("wpf", bi)], writes=[("wpb", bi)], tag=f"p3.wcv{oc}")
        for tc in range(4):
            sp = tc % 2
            for hc in range(16):
                S.op("tensor", lambda t, bi=bi, hc=hc, tc=tc, sp=sp: t.matmul(
                    SP[sp][:, :], wpb[bi][:, hc * 128:(hc + 1) * 128],
                    ctxin[:, hc, tc * 512:(tc + 1) * 512],
                    start=(hc == 0), stop=(hc == 15)),
                     reads=[("wpb", bi), ("ctxin", hc)], writes=[PS("sp%d" % sp)],
                     tag=f"p3.pmm{oc}.{tc}.{hc}")
            S.op("vector", lambda v, oc=oc, tc=tc, sp=sp: v.tensor_scalar(
                out=zfm[:, oc, tc * 512:(tc + 1) * 512], in0=SP[sp][:, :],
                scalar1=bp_sb[:, oc:oc + 1], scalar2=None, op0=ALU.add),
                 reads=[PS("sp%d" % sp), ("bp",)], writes=[("zfm", oc, tc)],
                 tag=f"p3.pev{oc}.{tc}")

    S.fence()

    for tt in range(16):
        xp = tt % 2
        S.op("sync", lambda e, tt=tt, xp=xp: e.dma_start(
            out=xres[xp][:, :], in_=xr[tt * 128:(tt + 1) * 128, :]),
             writes=[("xres", xp)], dma=True, tag=f"p3.xld{tt}")
        ATMKEYS = []
        for q in range(4):
            tp = q % 2
            ZK = [("zfm", 4 * q + j, tt // 4) for j in range(4)]
            S.op("vector", lambda v, q=q, tt=tt, tp=tp: v.tensor_copy(
                out=zstg[tp].rearrange("p (j f) -> p j f", j=4),
                in_=zfm[:, 4 * q:4 * q + 4, tt * 128:(tt + 1) * 128]),
                 reads=ZK, writes=[("zstg", tp)], tag=f"p3.zsg{tt}.{q}")
            for j in range(4):
                S.op("tensor", lambda t, j=j, tp=tp: t.transpose(
                    SP[tp][:, j * 128:(j + 1) * 128],
                    zstg[tp][:, j * 128:(j + 1) * 128], ident_f[:, :]),
                     reads=[("zstg", tp), ("identf",)], writes=[PS("sp%d" % tp)],
                     tag=f"p3.ztp{tt}.{q}.{j}")
            S.op("vector", lambda v, q=q, tp=tp: v.tensor_copy(
                out=atm[:, q * 512:(q + 1) * 512], in_=SP[tp][:, :]),
                 reads=[PS("sp%d" % tp)], writes=[("atm", q)], tag=f"p3.zev{tt}.{q}")
            ATMKEYS.append(("atm", q))
        st = stats[tt % 2]
        key = ln_stats(atm[:, :], st, "a" + str(tt), t1b, ATMKEYS, stbuf=tt % 2,
                       sqkey=("t1",))
        S.op("vector", lambda v, st=st: v.tensor_scalar(
            out=t1b[:, :], in0=atm[:, :], scalar1=st[:, 0:1], scalar2=st[:, 2:3],
            op0=ALU.subtract, op1=ALU.mult),
             reads=ATMKEYS + [key + (0,), key + (2,)], writes=[("t1",)],
             tag=f"p3.nrm{tt}")
        S.op("vector", lambda v, xp=xp: v.tensor_tensor(
            out=ybuf[:, :], in0=t1b[:, :], in1=xres[xp][:, :], op=ALU.add),
             reads=[("t1",), ("xres", xp)], writes=[("ybuf",)], tag=f"p3.yadd{tt}")
        S.op("scalar", lambda e, tt=tt: e.dma_start(
            out=y_scr[tt * 128:(tt + 1) * 128, :], in_=ybuf[:, :]),
             reads=[("ybuf",)], writes=[("yscr", tt)], dma=True, tag=f"p3.yst{tt}")
        key2 = ln_stats(ybuf[:, :], st, "y" + str(tt), t1b, [("ybuf",)], stoff=8,
                        stbuf=tt % 2, sqkey=("t1",))
        S.op("vector", lambda v, st=st: v.tensor_scalar(
            out=ln2tm[:, :], in0=ybuf[:, :], scalar1=st[:, 8:9], scalar2=st[:, 10:11],
            op0=ALU.subtract, op1=ALU.mult),
             reads=[("ybuf",), key2 + (0,), key2 + (2,)], writes=[("ln2tm",)],
             tag=f"p3.nrm2{tt}")
        for q in range(4):
            tp = q % 2
            for j in range(4):
                hc = q * 4 + j
                S.op("tensor", lambda t, hc=hc, j=j, tp=tp: t.transpose(
                    CT[tp][:, j * 128:(j + 1) * 128],
                    ln2tm[:, hc * 128:(hc + 1) * 128], ident_f[:, :]),
                     reads=[("ln2tm",), ("identf",)], writes=[PS("ct%d" % tp)],
                     tag=f"p3.l2tp{tt}.{hc}")
            S.op("vector", lambda v, q=q, tt=tt, tp=tp: v.tensor_copy(
                out=ln2fm[:, q * 4:(q + 1) * 4, tt * 128:(tt + 1) * 128],
                in_=CT[tp].rearrange("p (j f) -> p j f", j=4)),
                 reads=[PS("ct%d" % tp)], writes=[("ln2fm", tt)],
                 tag=f"p3.l2ev{tt}.{q}")

    S.fence()

    LN2ALL = [("ln2fm", tt) for tt in range(16)]
    for tp_ in range(4):
        for hc in range(16):
            S.op("tensor", lambda t, hc=hc, tp_=tp_: t.matmul(
                LP[0:1, :], ones_col[:, :], ln2fm[:, hc, tp_ * 512:(tp_ + 1) * 512],
                start=(hc == 0), stop=(hc == 15)),
                 reads=LN2ALL + [("onec",)], writes=[PS("lp")], tag=f"f.sr{tp_}.{hc}")
        S.op("vector", lambda v: v.tensor_copy(out=srow_f[0:1, :], in_=LP[0:1, :]),
             reads=[PS("lp")], writes=[("srowf",)], tag=f"f.src{tp_}")
        S.op("vector", lambda v: v.tensor_copy(out=srow_b[0:1, :], in_=srow_f[0:1, :]),
             reads=[("srowf",)], writes=[("srowb",)], tag=f"f.srb{tp_}")
        for ic in range(64):
            bi = ic % 2
            S.op("sync", lambda e, ic=ic, bi=bi: e.dma_start(
                out=gubuf[bi][:, :, :], in_=gu_scr[ic].rearrange("g p c -> p g c")),
                 reads=[("guscr", ic, 0), ("guscr", ic, 1)], writes=[("gbuf", bi)],
                 dma=True, tag=f"f.gld{tp_}.{ic}")
            for hc in range(16):
                S.op("tensor", lambda t, bi=bi, hc=hc, tp_=tp_: t.matmul(
                    SP[bi][:, :], gubuf[bi][:, 0, hc * 128:(hc + 1) * 128],
                    ln2fm[:, hc, tp_ * 512:(tp_ + 1) * 512],
                    start=(hc == 0), stop=False),
                     reads=[("gbuf", bi)] + LN2ALL, writes=[PS("sp%d" % bi)],
                     tag=f"f.gmm{tp_}.{ic}.{hc}")
            S.op("tensor", lambda t, bi=bi: t.matmul(
                SP[bi][:, :], m128row[:, :], srow_b[0:1, :], start=False, stop=True),
                 reads=[("m128",), ("srowb",)], writes=[PS("sp%d" % bi)],
                 tag=f"f.gcr{tp_}.{ic}")
            for hc in range(16):
                S.op("tensor", lambda t, bi=bi, hc=hc, tp_=tp_: t.matmul(
                    CT[bi][:, :], gubuf[bi][:, 1, hc * 128:(hc + 1) * 128],
                    ln2fm[:, hc, tp_ * 512:(tp_ + 1) * 512],
                    start=(hc == 0), stop=False),
                     reads=[("gbuf", bi)] + LN2ALL, writes=[PS("ct%d" % bi)],
                     tag=f"f.umm{tp_}.{ic}.{hc}")
            S.op("tensor", lambda t, bi=bi: t.matmul(
                CT[bi][:, :], m128row[:, :], srow_b[0:1, :], start=False, stop=True),
                 reads=[("m128",), ("srowb",)], writes=[PS("ct%d" % bi)],
                 tag=f"f.ucr{tp_}.{ic}")
            S.op("vector", lambda v, bi=bi, ic=ic: v.tensor_scalar(
                out=ustg[:, :], in0=CT[bi][:, :], scalar1=us_sb[:, ic:ic + 1],
                scalar2=None, op0=ALU.mult),
                 reads=[PS("ct%d" % bi), ("us",)], writes=[("ustg",)],
                 tag=f"f.uev{tp_}.{ic}")
            S.op("scalar", lambda s, bi=bi, ic=ic: s.activation(
                out=gstg[:, :], in_=SP[bi][:, :], func=AF.Silu,
                bias=0.0, scale=gs_sb[:, ic:ic + 1]),
                 reads=[PS("sp%d" % bi), ("gs",)], writes=[("gstg",)],
                 tag=f"f.gevs{tp_}.{ic}")
            S.op("vector", lambda v, ic=ic: v.tensor_tensor(
                out=h2[:, ic, :], in0=gstg[:, :], in1=ustg[:, :], op=ALU.mult),
                 reads=[("gstg",), ("ustg",)], writes=[("h2", ic)], tag=f"f.h2{tp_}.{ic}")
        accname = ["sp0", "sp1", "sp2", "sp3", "ct0", "ct1", "lp", "bc"]
        for tth in range(2):
            for ic in range(64):
                bi = ic % 2
                S.op("sync", lambda e, ic=ic, bi=bi: e.dma_start(
                    out=dbuf[bi][:, :], in_=dbf_scr[ic]),
                     reads=[("dscr", ic)], writes=[("dbuf", bi)], dma=True,
                     tag=f"f.dld{tp_}.{tth}.{ic}")
                for t3 in range(2):
                    tloc = tth * 2 + t3
                    for hq in range(4):
                        ai = t3 * 4 + hq
                        S.op("tensor", lambda t, ic=ic, bi=bi, tloc=tloc, hq=hq,
                             ai=ai: t.matmul(
                            ACC8[ai][:, :], h2[:, ic, tloc * 128:(tloc + 1) * 128],
                            dbuf[bi][:, hq * 512:(hq + 1) * 512],
                            start=(ic == 0), stop=(ic == 63)),
                             reads=[("h2", ic), ("dbuf", bi)],
                             writes=[PS(accname[ai])],
                             tag=f"f.dmm{tp_}.{tth}.{ic}.{tloc}.{hq}")
            for t3 in range(2):
                for hq in range(4):
                    ai = t3 * 4 + hq
                    S.op("vector", lambda v, t3=t3, hq=hq, ai=ai: v.tensor_copy(
                        out=ftm[t3][:, hq * 512:(hq + 1) * 512], in_=ACC8[ai][:, :]),
                         reads=[PS(accname[ai])], writes=[("ftm", t3, hq)],
                         tag=f"f.dev{tp_}.{tth}.{t3}.{hq}")
            for t3 in range(2):
                gt = tp_ * 4 + tth * 2 + t3
                FKEYS = [("ftm", t3, i) for i in range(4)]
                st = stats[t3]
                key = ln_stats(ftm[t3][:, :], st, f"f{tp_}_{tth}_{t3}", outb, FKEYS,
                               stbuf=t3, sqkey=("outb",))
                S.op("sync", lambda e, gt=gt: e.dma_start(
                    out=yresb[:, :], in_=y_scr[gt * 128:(gt + 1) * 128, :]),
                     reads=[("yscr", gt)], writes=[("yresb",)], dma=True,
                     tag=f"f.yld{gt}")
                S.op("vector", lambda v, t3=t3, st=st: v.tensor_scalar(
                    out=outb[:, :], in0=ftm[t3][:, :], scalar1=st[:, 0:1],
                    scalar2=st[:, 2:3], op0=ALU.subtract, op1=ALU.mult),
                     reads=FKEYS + [key + (0,), key + (2,)], writes=[("outb",)],
                     tag=f"f.nrm{gt}")
                S.op("vector", lambda v: v.tensor_tensor(
                    out=outb[:, :], in0=outb[:, :], in1=yresb[:, :], op=ALU.add),
                     reads=[("outb",), ("yresb",)], writes=[("outb",)],
                     tag=f"f.oadd{gt}")
                S.op("scalar", lambda e, gt=gt: e.dma_start(
                    out=y_out[gt * 128:(gt + 1) * 128, :], in_=outb[:, :]),
                     reads=[("outb",)], writes=[("yout", gt)], dma=True,
                     tag=f"f.ost{gt}")

    S.op("gpsimd", lambda e: e.engine_nop(),
         reads=[("yout", gt) for gt in range(16)], tag="tail")

    with ExitStack() as ctx:
        sems_c = {e: ctx.enter_context(nc.semaphore(f"semc_{e}")) for e in ENGS}
        sems_d = {e: [ctx.enter_context(nc.semaphore(f"semd_{e}{i}"))
                      for i in range(8 if e == "sync" else 4)]
                  for e in ENGS}
        ccsem = [ctx.enter_context(nc.semaphore(f"sem_cc{i}")) for i in range(2)]
        block = ctx.enter_context(nc.Block())
        S.emit(block, sems_c, sems_d, ccsem)
    return nc


def kernel(L0_attn_query, L0_attn_key, L0_attn_value, L0_attn_Wqkv,
           L0_attn_Wproj, L0_attn_bproj, L0_ffn_gate_w, L0_ffn_gate_scale,
           L0_ffn_up_w, L0_ffn_up_scale, L0_ffn_down_w, L0_ffn_down_scale):
    _ = (L0_attn_key, L0_attn_value)
    x = np.ascontiguousarray(np.asarray(L0_attn_query, dtype=np.float32))
    Wqkv = np.asarray(L0_attn_Wqkv, dtype=np.float32)
    Wproj = np.asarray(L0_attn_Wproj, dtype=np.float32)
    bproj = np.asarray(L0_attn_bproj, dtype=np.float32)
    gw = np.asarray(L0_ffn_gate_w, dtype=np.int32)
    uw = np.asarray(L0_ffn_up_w, dtype=np.int32)
    dw = np.asarray(L0_ffn_down_w, dtype=np.int32)
    gsc = np.asarray(L0_ffn_gate_scale, dtype=np.float32)
    usc = np.asarray(L0_ffn_up_scale, dtype=np.float32)
    dsc = np.asarray(L0_ffn_down_scale, dtype=np.float32)

    wq_all = np.ascontiguousarray(
        Wqkv.reshape(16, 3, 128, 16, 128).transpose(0, 1, 4, 3, 2))
    wp_st = np.ascontiguousarray(
        Wproj.reshape(16, 128, 16, 128).transpose(0, 3, 2, 1))
    gw_st = np.ascontiguousarray(
        gw.reshape(64, 128, 16, 128).transpose(0, 3, 2, 1))
    uw_st = np.ascontiguousarray(
        uw.reshape(64, 128, 16, 128).transpose(0, 3, 2, 1))
    dw_st = np.ascontiguousarray(dw.T).reshape(64, 128, 2048)
    gs_st = np.ascontiguousarray(gsc.reshape(64, 128).T)
    us_st = np.ascontiguousarray(usc.reshape(64, 128).T)
    ds_st = np.ascontiguousarray(dsc.reshape(1, 2048))
    bp_st = np.ascontiguousarray(bproj.reshape(16, 128).T)

    in_maps = []
    for c in range(NCORE):
        bg, sh = c // 2, c % 2
        hset = list(range(8 * (c % 2), 8 * (c % 2) + 8))
        cgi = np.empty((128, 16), np.int32)
        for n_ in range(16):
            g = 4 * n_ + bg
            r_src = 2 * (g // 16) + (1 if (g % 16) >= 8 else 0)
            slot_h = (g % 16) % 4
            row0 = r_src * 512 + slot_h * 128
            cgi[:, n_] = (row0 + np.arange(128)) * 2 + sh
        in_maps.append({
            "xs": np.ascontiguousarray(x[bg]),
            "xr": np.ascontiguousarray(x[bg, sh * 2048:(sh + 1) * 2048]),
            "wq_st": np.ascontiguousarray(wq_all[hset]),
            "wp_st": wp_st,
            "gw_st": gw_st,
            "uw_st": uw_st,
            "dw_st": dw_st,
            "gs_in": gs_st,
            "us_in": us_st,
            "ds_in": ds_st,
            "bp_in": bp_st,
            "cg_in": cgi,
        })

    nc = build_program()
    res = run_bass_kernel_spmd(nc, in_maps, core_ids=list(range(NCORE)))
    out = np.empty((B, SQ, H), np.float32)
    for c in range(NCORE):
        bg, sh = c // 2, c % 2
        out[bg, sh * 2048:(sh + 1) * 2048, :] = res.results[c]["y_out"]
    return out
